# version 02
# baseline (speedup 1.0000x reference)
import contextlib
import numpy as np
import ml_dtypes

import concourse.bass as bass
import concourse.mybir as mybir
import bass_rust
from concourse.bass_utils import run_bass_kernel_spmd

F32 = mybir.dt.float32
BF16 = mybir.dt.bfloat16
U8 = mybir.dt.uint8
AF = mybir.ActivationFunctionType
ALU = mybir.AluOpType
AX = mybir.AxisListType

T = 2048
NT = 16
NCH = 8
D = 1024
NFC = 8
NL = 6
DI = 2730
W_FFN = 390
NFFC = 7
MASKVAL = -1e9

DBG = {}
DBGC = 0



def host_consts():
    c = {}
    DH = 64
    WIN = 256
    half = np.arange(0, DH, 2, dtype=np.float64)
    inv_freq = 1.0 / 10000.0 ** (half / DH)
    t = np.arange(T, dtype=np.float64)
    freqs = t[:, None] * inv_freq[None, :]
    freqs = np.concatenate([freqs, freqs], -1)
    base_scale = (half + 0.4 * DH) / (1.4 * DH)
    power = (t - T // 2) / (WIN // 2)
    scale = base_scale[None, :] ** power[:, None]
    scale = np.concatenate([scale, scale], -1)
    cos = np.cos(freqs)
    sin = np.sin(freqs)
    sgn = np.where(np.arange(DH) < 32, -1.0, 1.0)[None, :]
    c["tab_Aq"] = (cos * scale).astype(np.float32)
    c["tab_Bq"] = (sgn * sin * scale).astype(np.float32)
    c["tab_Ak"] = (cos / scale).astype(np.float32)
    c["tab_Bk"] = (sgn * sin / scale).astype(np.float32)

    i = np.arange(128)[:, None]
    w = np.arange(384)[None, :]
    m = np.zeros((2, 3, 128, 384), dtype=np.float32)
    for kind in range(2):
        lo = i + (1 - kind)
        valid = (w >= lo) & (w <= i + 256)
        for var in range(3):
            vv = valid.copy()
            if var == 1:
                vv &= w >= 256
            if var == 2:
                vv &= w >= 128
            m[kind, var] = np.where(vv, 0.0, MASKVAL)
    c["masks_win"] = m.astype(ml_dtypes.bfloat16)

    p = np.arange(128)[:, None]
    n = np.arange(64)[None, :]
    lm = np.zeros((NT, 128, 64), dtype=np.float32)
    for tt in range(NT):
        lm[tt] = np.where(n <= (tt * 128 + p) // 32, 0.0, MASKVAL)
    c["masks_lat"] = lm.astype(ml_dtypes.bfloat16)

    pp = np.zeros((NT, 128, 64), dtype=np.float32)
    for tt in range(NT):
        pp[tt, np.arange(128), (tt * 128 + np.arange(128)) // 32] = 1.0 / 32.0
    c["ppool"] = pp.astype(ml_dtypes.bfloat16)

    c["ident_bf"] = np.eye(128, dtype=ml_dtypes.bfloat16)
    c["ones2048"] = np.ones((1, T), dtype=ml_dtypes.bfloat16)
    c["ident_f32"] = np.eye(128, dtype=np.float32)
    c["ones_row"] = np.ones((1, 128), dtype=np.float32)
    return c


def lay_b1(b1):
    out = np.zeros((NL, 128, 56), dtype=np.float32)
    for L in range(NL):
        col = 0
        for half in range(2):
            base = half * DI
            for ch in range(NFFC):
                for s in range(4):
                    lo = ch * W_FFN + s * 128
                    hi = min(ch * W_FFN + W_FFN, lo + 128)
                    if lo >= ch * W_FFN + W_FFN:
                        continue
                    nrow = hi - lo
                    out[L, :nrow, col] = b1[L, base + lo:base + hi]
                    col += 1
    return out



class Sem:
    def __init__(self, h, name):
        self.h = h
        self.name = name
        self.n = 0


class K:
    ENGINES = ("pe", "act", "dve", "sp", "pool")

    def __init__(self, nc):
        self.nc = nc
        self.q = {e: [] for e in self.ENGINES}
        self.sems = {}
        self.es = contextlib.ExitStack()
        self.barrier_sem = None
        self.nbarrier = 0

    def sem(self, name):
        s = Sem(self.es.enter_context(self.nc.semaphore(name)), name)
        self.sems[name] = s
        return s

    def emit(self, eng, fn, wait=None, inc=None):
        if wait:
            for s, v in wait:
                self.q[eng].append(("w", s, v))
        if inc:
            s, n = inc
            s.n += n
            self.q[eng].append(("i", fn, s, n))
        else:
            self.q[eng].append(("f", fn))

    def barrier(self, engines=None):
        self.nbarrier += 1
        sems = [self.s_pe, self.s_dv, self.s_ac, self.s_po, self.s_st,
                self.s_xl[0], self.s_xl[1]]
        for e in self.ENGINES:
            for s in sems:
                if s.n > 0:
                    self.q[e].append(("w", s, s.n))

    def build_block(self):
        nc = self.nc
        with nc.Block() as block:
            for eng, reg in (("pe", block.tensor), ("act", block.scalar),
                             ("dve", block.vector), ("sp", block.sync),
                             ("pool", block.gpsimd)):
                items = self.q[eng]
                def body(h, items=items):
                    for it in items:
                        if it[0] == "w":
                            h.wait_ge(it[1].h, it[2])
                        elif it[0] == "f":
                            it[1](h)
                        else:
                            instr = it[1](h)
                            instr.then_inc(it[2].h, it[3])
                reg(body)


def _nop(h):
    return h.nop()


def ap_view(t, offset_elems, ap_pairs):
    v = t.copy()
    v.ap = bass_rust.VecI64Pair(ap_pairs)
    v.offset = v.offset + offset_elems
    return v



class MK(K):
    def __init__(self, nc, dbg=()):
        super().__init__(nc)
        self.dbg_names = set(dbg)
        self.dbg_out = {}
        es = self.es

        dt = nc.dram_tensor
        self.x_in = dt("x_in", [2, T, D], F32, kind="ExternalInput").ap()
        self.out = dt("out", [2, T, D], F32, kind="ExternalOutput").ap()
        wnames = {
            "wq": [D, D], "wk": [D, D], "wv": [D, D],
            "wgq": [D, 512], "wgk": [D, 512], "wgv": [D, 512],
            "r_out_w": [512, D], "glw": [16, 64], "grw": [16, 64], "w_out": [D, D],
            "Wqkv": [NL, D, 3072], "qs": [NL, 64], "ks": [NL, 64],
            "Wo": [NL, D, D], "ln_g": [NL, D], "ln_b": [NL, D],
            "W1": [NL, D, 2 * DI], "b1l": [NL, 128, 56], "W2": [NL, DI, D],
            "b2": [NL, D], "gw_bd": [D, 32],
        }
        self.w = {k: dt(k, shp, F32, kind="ExternalInput").ap() for k, shp in wnames.items()}
        cshapes = {
            "tab_Aq": ([T, 64], F32), "tab_Bq": ([T, 64], F32),
            "tab_Ak": ([T, 64], F32), "tab_Bk": ([T, 64], F32),
            "masks_win": ([2, 3, 128, 384], BF16), "masks_lat": ([NT, 128, 64], BF16),
            "ppool": ([NT, 128, 64], BF16), "ident_bf": ([128, 128], BF16),
            "ident_f32": ([128, 128], F32), "ones_row": ([1, 128], F32),
            "ones2048": ([1, T], BF16),
        }
        self.c = {k: dt(k, shp, d, kind="ExternalInput").ap() for k, (shp, d) in cshapes.items()}
        self.x1_scr = dt("x1_scr", [T, D], F32, kind="Internal").ap()
        self.local_scr = dt("local_scr", [T, D], BF16, kind="Internal").ap()
        self.remT_scr = dt("remT_scr", [4, 128, T], BF16, kind="Internal").ap()

        sb = lambda name, shape, dtp: es.enter_context(nc.sbuf_tensor(name, shape, dtp))
        self.y = sb("y_res", [128, NT * D], F32)
        self.masksW = sb("masksW_sb", [128, 6 * 384], BF16)
        self.masksL = sb("masksL_sb", [128, NT * 64], BF16)
        self.ppool = sb("ppool_sb", [128, NT * 64], BF16)
        self.identB = sb("identB_sb", [128, 128], BF16)
        self.identF = sb("identF_sb", [128, 128], F32)
        self.ones_row = sb("ones_row_sb", [1, 128], F32)
        self.stats_mla = sb("stats_mla", [128, NT * 2], F32)
        self.qs_rep = sb("qs_rep", [128, 64], F32)
        self.ks_rep = sb("ks_rep", [128, 64], F32)
        self.glgr = sb("glgr", [128, NT * 32], F32)
        self.cols = sb("cols", [128, 64], F32)
        self.AR = 122368
        self.arena = sb("arena", [128, self.AR], U8)

        self.ps = es.enter_context(nc.psum_tensor("ps", [128, 8 * 512], F32))

        self.s_pe = self.sem("pe")
        self.s_dv = self.sem("dv")
        self.s_ac = self.sem("ac")
        self.s_po = self.sem("po")
        self.s_sp = self.sem("sp")
        self.s_wl = [self.sem(f"wl{i}") for i in range(6)]
        self._wl_i = 0
        self.s_sl = [self.sem(f"sl{i}") for i in range(3)]
        self._sl_i = 0
        self.s_xl = [self.sem("xl0"), self.sem("xl1")]
        self.s_st = self.sem("st")
        self.s_cl = self.sem("cl")

    def bank(self, b, lo=0, hi=512):
        return self.ps[:, b * 512 + lo:b * 512 + hi]

    def bankb(self, b, lo=0, hi=1024):
        return self.ps[:, b * 512:(b + 1) * 512].bitcast(BF16)[:, lo:hi]

    def ar(self, off, dtp, n):
        nb = n * (4 if dtp == F32 else 2 if dtp in (BF16,) else 1)
        assert off + nb <= self.AR, (off, nb)
        return self.arena[:, off:off + nb].bitcast(dtp)

    def ytile(self, t):
        return self.y[:, t * D:(t + 1) * D]

    def pe(self, fn, wait=None, inc=None):
        self.emit("pe", fn, wait, inc)

    def act(self, fn, wait=None, inc=None):
        self.emit("act", fn, wait, inc)

    def dve(self, fn, wait=None, inc=None):
        self.emit("dve", fn, wait, inc)

    def sp(self, fn, wait=None, inc=None):
        self.emit("sp", fn, wait, inc)

    def pool(self, fn, wait=None, inc=None):
        self.emit("pool", fn, wait, inc)

    def dump(self, name, view, wait):
        if name not in self.dbg_names:
            return
        shp = list(view.shape)
        d = self.nc.dram_tensor("dbg_" + name, shp, view.dtype, kind="ExternalOutput").ap()
        self.dbg_out[name] = shp
        self.sp(lambda h, d=d, v=view: h.dma_start(out=d, in_=v),
                wait=wait, inc=(self.s_st, 16))
        for e in self.ENGINES:
            self.q[e].append(("w", self.s_st, self.s_st.n))

    def wl_next(self):
        s = self.s_wl[self._wl_i % 6]
        self._wl_i += 1
        return s

    def sl_next(self):
        s = self.s_sl[self._sl_i % 3]
        self._sl_i += 1
        return s

    def load_consts(self):
        c, cl = self.c, self.s_cl
        loads = [
            (self.masksW[:].rearrange("p (a b n) -> p a b n", a=2, b=3), c["masks_win"].rearrange("a b p n -> p a b n")),
            (self.masksL[:].rearrange("p (t n) -> p t n", n=64), c["masks_lat"].rearrange("t p n -> p t n")),
            (self.ppool[:].rearrange("p (t n) -> p t n", n=64), c["ppool"].rearrange("t p n -> p t n")),
            (self.identB, c["ident_bf"]),
            (self.identF, c["ident_f32"]),
            (self.ones_row, c["ones_row"]),
        ]
        self.dve(lambda h: h.memset(self.cols[:, 3:4], 1e-5))
        for dst, src in loads:
            self.sp(lambda h, d=dst, s=src: h.dma_start(
                out=(d[:, :] if hasattr(d, "free_size") is False else d), in_=s), inc=(cl, 16))
        for e in self.ENGINES:
            self.q[e].append(("w", cl, cl.n))

    def phase0(self, s):
        nc = self.nc
        A_W1 = 49152
        wgk = self.ar(A_W1, BF16, 8 * 512).rearrange("p (f o) -> p f o", o=512)
        wgv = self.ar(A_W1 + 8192, BF16, 8 * 512).rearrange("p (f o) -> p f o", o=512)
        xst = self.ar(111104, F32, 2 * 1024).rearrange("p (s f) -> p s f", f=D)
        xn1 = self.ar(86528, BF16, 2 * 1024).rearrange("p (s f) -> p s f", f=D)
        junk = self.ar(107008, BF16, 1024)
        junk2 = self.ar(109056, BF16, 1024)
        xp_bf = self.ar(65536, BF16, 1024)
        xpT = self.ar(65536 + 2048, BF16, 8 * 64).rearrange("p (f n) -> p f n", n=64)
        GK = self.ar(65536 + 3072, BF16, 4 * 64).rearrange("p (g n) -> p g n", n=64)
        GV = self.ar(65536 + 3584, BF16, 512)
        self.p0 = dict(xp_bf=xp_bf, xpT=xpT, GK=GK, GV=GV)

        wl = self.wl_next()
        self.pool(lambda h: h.dma_start(out=wgk[:, :, :],
                  in_=self.w["wgk"].rearrange("(f p) o -> p f o", p=128)), inc=(wl, 16))
        self.pool(lambda h: h.dma_start(out=wgv[:, :, :],
                  in_=self.w["wgv"].rearrange("(f p) o -> p f o", p=128)), inc=(wl, 16))
        self._p0_wl_t = wl.n

        ps, iB = self.ps, self.identB
        pe, dv, ac = self.s_pe, self.s_dv, self.s_ac
        ST = self.stats_mla
        for t in range(NT):
            sl = t % 2
            xl = self.s_xl[sl]
            w = []
            if t >= 2:
                w = [(dv, self._dv_apply[t - 2]), (ac, self._ac_sq[t - 2])] if hasattr(self, "_dv_apply") else []
            self.sp(lambda h, t=t, sl=sl: h.dma_start(
                out=xst[:, sl, :], in_=self.x_in[s, t * 128:(t + 1) * 128, :]),
                wait=w, inc=(xl, 16))
            if t == 0:
                self._dv_apply = {}
                self._ac_sq = {}
            self.dve(lambda h, sl=sl, t=t: h.tensor_scalar(
                out=junk[:, :], in0=xst[:, sl, :], scalar1=-1.0 / D, scalar2=None,
                op0=ALU.mult, op1=ALU.add, accum_out=ST[:, 2 * t:2 * t + 1]),
                wait=[(xl, xl.n)], inc=(dv, 1))
            negmu_done = dv.n
            self.act(lambda h, sl=sl, t=t: h.activation(
                out=junk2[:, :], in_=xst[:, sl, :], func=AF.Square,
                bias=ST[:, 2 * t:2 * t + 1], accum_out=self.cols[:, 1:2]),
                wait=[(dv, negmu_done)], inc=(ac, 1))
            sq_acc = ac.n
            self.act(lambda h: h.activation(
                out=self.cols[:, 2:3], in_=self.cols[:, 1:2], func=AF.Sqrt,
                scale=1.0 / D, bias=self.cols[:, 3:4]),
                wait=[(ac, sq_acc)], inc=(ac, 1))
            self._ac_sq[t] = ac.n
            sq_done = ac.n
            self.dve(lambda h, t=t: h.reciprocal(
                out=ST[:, 2 * t + 1:2 * t + 2], in_=self.cols[:, 2:3]),
                wait=[(ac, sq_done)], inc=(dv, 1))
            rstd_done = dv.n
            w2 = [(pe, self._p0mm[t - 2])] if t >= 2 else []
            self.dve(lambda h, sl=sl, t=t: h.tensor_scalar(
                out=xn1[:, sl, :], in0=xst[:, sl, :],
                scalar1=ST[:, 2 * t:2 * t + 1], scalar2=ST[:, 2 * t + 1:2 * t + 2],
                op0=ALU.add, op1=ALU.mult),
                wait=w2 + [(dv, rstd_done)], inc=(dv, 1))
            self._dv_apply[t] = dv.n
            apl = dv.n
            if t == 0:
                self._p0mm = {}
            self.pe(lambda h, t=t, sl=sl: h.matmul(
                self.bank(6)[0:64, :], self.ppool[:, t * 64:(t + 1) * 64], xn1[:, sl, 0:512],
                start=(t == 0), stop=(t == NT - 1), skip_group_check=True),
                wait=[(dv, apl)] + ([(self.s_wl[0], 0)] if False else []))
            self.pe(lambda h, t=t, sl=sl: h.matmul(
                self.bank(7)[0:64, :], self.ppool[:, t * 64:(t + 1) * 64], xn1[:, sl, 512:1024],
                start=(t == 0), stop=(t == NT - 1), skip_group_check=True),
                inc=(pe, 1))
            self._p0mm[t] = pe.n

        pool_done = pe.n
        self.act(lambda h: h.copy(out=xp_bf[0:64, 0:512], in_=self.bank(6)[0:64, :]),
                 wait=[(ac, 0), (self.s_pe, pool_done)])
        self.act(lambda h: h.copy(out=xp_bf[0:64, 512:1024], in_=self.bank(7)[0:64, :]),
                 inc=(ac, 1))
        xp_done = ac.n
        self.dump("xp", xp_bf[0:64, :], [(ac, xp_done)])
        for j in range(8):
            self.pe(lambda h, j=j: h.transpose(
                self.bankb(0, j * 64, (j + 1) * 64)[0:128, :], xp_bf[0:64, j * 128:(j + 1) * 128], iB[0:64, 0:64]),
                wait=([(ac, xp_done)] if j == 0 else None),
                inc=((pe, 1) if j == 7 else None))
        xpT_t = pe.n
        self.dve(lambda h: h.tensor_copy(out=xpT[:, :, :], in_=self.bankb(0, 0, 512).rearrange("p (f n) -> p f n", n=64)),
                 wait=[(self.s_pe, xpT_t)], inc=(dv, 1))
        xpT_done = dv.n
        wlv = self.s_wl[(self._wl_i - 1) % 6]
        for oc in range(4):
            for fc in range(8):
                self.pe(lambda h, oc=oc, fc=fc: h.matmul(
                    self.bank(1, oc * 64, (oc + 1) * 64)[0:128, :].rearrange("p n -> p n"),
                    wgk[:, fc, oc * 128:(oc + 1) * 128], xpT[:, fc, :],
                    start=(fc == 0), stop=(fc == 7), skip_group_check=True),
                    wait=([(dv, xpT_done), (wlv, self._p0_wl_t)] if (oc == 0 and fc == 0) else None))
        for fc in range(8):
            self.pe(lambda h, fc=fc: h.matmul(
                self.bank(2)[0:64, :], xpT[:, fc, :], wgv[:, fc, :],
                start=(fc == 0), stop=(fc == 7)),
                inc=((pe, 1) if fc == 7 else None))
        gkv_done = pe.n
        self._gkv_done = gkv_done
        self.act(lambda h: h.copy(out=GK[:, :, :], in_=self.bank(1, 0, 256).rearrange("p (g n) -> p g n", n=64)),
                 wait=[(self.s_pe, gkv_done)])
        self.act(lambda h: h.copy(out=GV[0:64, :], in_=self.bank(2)[0:64, :]), inc=(ac, 1))
        self.dump("GK", GK[:, :, :], [(ac, ac.n)])
        self.dump("GV", GV[0:64, :], [(ac, ac.n)])
        self.dump("stats_mla", ST[:, :], [(dv, dv.n)])

    def mla(self, s):
        nc = self.nc
        pe, dv, ac, sp_, po = self.s_pe, self.s_dv, self.s_ac, self.s_po, self.s_sp
        ST, iB = self.stats_mla, self.identB
        wq = self.ar(0, BF16, 8 * 1024).rearrange("p (f o) -> p f o", o=1024)
        wk = self.ar(16384, BF16, 8 * 1024).rearrange("p (f o) -> p f o", o=1024)
        wv = self.ar(32768, BF16, 8 * 1024).rearrange("p (f o) -> p f o", o=1024)
        wgq = self.ar(49152, BF16, 8 * 512).rearrange("p (f o) -> p f o", o=512)
        gw_bd = self.ar(57344, BF16, 8 * 32).rearrange("p (f o) -> p f o", o=32)
        GQTc = self.ar(65536, BF16, 4 * 256).rearrange("p (g j) -> p g j", j=256)
        GK = self.p0["GK"]
        GV = self.p0["GV"]
        qTc = self.ar(65536 + 8704, BF16, 8 * 256).rearrange("p (f j) -> p f j", j=256)
        remTc = self.ar(65536 + 12800, BF16, 4 * 256).rearrange("p (g j) -> p g j", j=256)
        Pbuf = self.ar(65536 + 14848, BF16, 2 * 384).rearrange("p (h j) -> p h j", j=384)
        xn1c = self.ar(81920, BF16, 8 * 256).rearrange("p (f j) -> p f j", j=256)
        xtok = self.ar(86528, BF16, 2 * 1024).rearrange("p (s f) -> p s f", f=D)
        kT = self.ar(90624, BF16, 8 * 4 * 128).rearrange("p (f g j) -> p f g j", g=4, j=128)
        vR = self.ar(98816, BF16, 4 * 1024).rearrange("p (g f) -> p g f", f=D)
        lcl = self.ar(107008, BF16, 2 * 1024).rearrange("p (q f) -> p q f", f=D)
        xst = self.ar(111104, F32, 2 * 1024).rearrange("p (s f) -> p s f", f=D)
        PT = self.ar(119296, BF16, 2 * 3 * 128).rearrange("p (h b j) -> p h b j", b=3, j=128)

        CL = self.cols

        wl_qkv = self.wl_next()
        for dst, name in ((wq, "wq"), (wk, "wk"), (wv, "wv")):
            self.pool(lambda h, d=dst, n=name: h.dma_start(
                out=d[:, :, :], in_=self.w[n].rearrange("(f p) o -> p f o", p=128)),
                inc=(wl_qkv, 16))
        wl_qkv_t = wl_qkv.n
        wl_g = self.wl_next()
        self.pool(lambda h: h.dma_start(
            out=wgq[:, :, :], in_=self.w["wgq"].rearrange("(f p) o -> p f o", p=128)),
            wait=[(pe, self._gkv_done)], inc=(wl_g, 16))
        self.pool(lambda h: h.dma_start(
            out=gw_bd[:, :, :], in_=self.w["gw_bd"].rearrange("(f p) o -> p f o", p=128)),
            inc=(wl_g, 16))
        wl_g_t = wl_g.n

        self.pool(lambda h: h.memset(kT[:, :, 2:4, :], 0.0))
        self.pool(lambda h: h.memset(vR[:, 2:4, :], 0.0), inc=(po, 1))
        ring_z = po.n

        xn1T_done = {}
        q_evac = {}
        pv_evac = {}
        chunk_last_pe = {}
        apply_done = {}
        exp_done_prev = 0
        ptevac_prev = 0
        prev_kv_evac = 0
        mm_gl_done = {}
        lcl_st = {}
        rem_st = {}

        for c in range(NCH):
            t0, t1 = 2 * c, 2 * c + 1
            for t in (t0, t1):
                sl = t % 2
                xl = self.s_xl[sl]
                w = []
                if t >= 2:
                    w.append((dv, apply_done.get(t - 2, 0)))
                self.sp(lambda h, t=t, sl=sl: h.dma_start(
                    out=xst[:, sl, :], in_=self.x_in[s, t * 128:(t + 1) * 128, :]),
                    wait=w, inc=(xl, 16))
            for ti, t in enumerate((t0, t1)):
                sl = t % 2
                xl = self.s_xl[sl]
                w = [(xl, xl.n)]
                if c >= 1:
                    w.append((pe, self._qT_mm.get(t - 2, 0)))
                self.dve(lambda h, sl=sl, t=t: h.tensor_scalar(
                    out=xtok[:, sl, :], in0=xst[:, sl, :],
                    scalar1=ST[:, 2 * t:2 * t + 1], scalar2=ST[:, 2 * t + 1:2 * t + 2],
                    op0=ALU.add, op1=ALU.mult),
                    wait=w, inc=(dv, 1))
                apply_done[t] = dv.n
                bnk = 6 + ti
                wpe = [(dv, apply_done[t])]
                if c >= 1:
                    wpe.append((dv, xn1T_done.get(t - 2, 0)))
                    wpe.append((ac, ptevac_prev))
                for j in range(8):
                    self.pe(lambda h, sl=sl, j=j, bnk=bnk: h.transpose(
                        self.bankb(bnk, j * 128, (j + 1) * 128), xtok[:, sl, j * 128:(j + 1) * 128], iB[:, :]),
                        wait=(wpe if j == 0 else None), inc=((pe, 1) if j == 7 else None))
                tT = pe.n
                self.dve(lambda h, ti=ti, bnk=bnk: h.tensor_copy(
                    out=ap_view(xn1c[:, 0, 0:128], ti * 128, [[xn1c.tensor.shape[-1] if False else 0, 0]]) if False else
                    xn1c[:, :, ti * 128:(ti + 1) * 128],
                    in_=self.bankb(bnk, 0, 1024).rearrange("p (f j) -> p f j", j=128)),
                    wait=[(pe, tT)], inc=(dv, 1))
                xn1T_done[t] = dv.n

            if c == 0:
                self._qT_mm = {}
            wl_w = [(wl_qkv, wl_qkv_t)] if c == 0 else []
            kv_evacs = []
            for oc in range(8):
                bslot = (4 + (oc % 2), 256 * ((oc // 2) % 2))
                wk_w = list(wl_w) if oc == 0 else []
                wk_w.append((dv, xn1T_done[t1]))
                if oc >= 4:
                    wk_w.append((ac, kv_evacs[oc - 4]))
                for fc in range(8):
                    self.pe(lambda h, oc=oc, fc=fc, b=bslot: h.matmul(
                        self.bank(b[0], b[1], b[1] + 256), wk[:, fc, oc * 128:(oc + 1) * 128],
                        xn1c[:, fc, :], start=(fc == 0), stop=(fc == 7)),
                        wait=(wk_w if fc == 0 else None), inc=((pe, 1) if fc == 7 else None))
                mmk = pe.n
                s0 = (2 * c) % 4
                self.act(lambda h, oc=oc, b=bslot, s0=s0: h.copy(
                    out=kT[:, oc, s0:s0 + 2, :].rearrange("p g j -> p (g j)"),
                    in_=self.bank(b[0], b[1], b[1] + 256)),
                    wait=[(pe, mmk)], inc=(ac, 1))
                kv_evacs.append(ac.n)
            for ti, t in enumerate((t0, t1)):
                sl = t % 2
                bA = 2 * ti
                wq_w = [(dv, xn1T_done[t])]
                if c >= 1:
                    wq_w.append((ac, pv_evac.get(2 * (c - 1) + ti, 0)))
                for fc in range(8):
                    for g in range(2):
                        self.pe(lambda h, fc=fc, g=g, ti=ti, bA=bA: h.matmul(
                            self.bank(bA + g), xn1c[:, fc, ti * 128:(ti + 1) * 128],
                            wq[:, fc, g * 512:(g + 1) * 512],
                            start=(fc == 0), stop=(fc == 7)),
                            wait=(wq_w if (fc == 0 and g == 0) else None),
                            inc=((pe, 1) if (fc == 7 and g == 1) else None))
                mmq = pe.n
                self.act(lambda h, sl=sl, bA=bA: h.activation(
                    out=xtok[:, sl, :], in_=self.ps[:, bA * 512:(bA + 2) * 512],
                    func=AF.Copy, scale=0.125, bias=0.0),
                    wait=[(pe, mmq)], inc=(ac, 1))
                q_evac[t] = ac.n
                bnk = 6 + ti
                wq2 = [(ac, q_evac[t]), (dv, xn1T_done[t1])]
                for j in range(8):
                    self.pe(lambda h, sl=sl, j=j, bnk=bnk: h.transpose(
                        self.bankb(bnk, j * 128, (j + 1) * 128), xtok[:, sl, j * 128:(j + 1) * 128], iB[:, :]),
                        wait=(wq2 if j == 0 else None), inc=((pe, 1) if j == 7 else None))
                self._qT_mm[t] = pe.n
                tq = pe.n
                self.dve(lambda h, ti=ti, bnk=bnk: h.tensor_copy(
                    out=qTc[:, :, ti * 128:(ti + 1) * 128],
                    in_=self.bankb(bnk, 0, 1024).rearrange("p (f j) -> p f j", j=128)),
                    wait=[(pe, tq)], inc=(dv, 1))
                qT_done = dv.n
                glw_w = [(dv, qT_done)] + ([(wl_g, wl_g_t)] if (c == 0 and ti == 0) else [])
                for fc in range(8):
                    self.pe(lambda h, fc=fc, ti=ti, bA=bA: h.matmul(
                        self.bank(bA, 0, 32), qTc[:, fc, ti * 128:(ti + 1) * 128],
                        gw_bd[:, fc, :], start=(fc == 0), stop=(fc == 7)),
                        wait=(glw_w if fc == 0 else None), inc=((pe, 1) if fc == 7 else None))
                mm_gl = pe.n
                self.act(lambda h, t=t, bA=bA: h.activation(
                    out=self.glgr[:, t * 32:(t + 1) * 32], in_=self.bank(bA, 0, 32),
                    func=AF.Copy, scale=8.0, bias=0.0),
                    wait=[(pe, mm_gl)], inc=(ac, 1))
                mm_gl_done[t] = ac.n
                wv_w = [(ac, mm_gl_done[t])]
                for fc in range(8):
                    for g in range(2):
                        self.pe(lambda h, fc=fc, g=g, ti=ti, bA=bA: h.matmul(
                            self.bank(bA + g), xn1c[:, fc, ti * 128:(ti + 1) * 128],
                            wv[:, fc, g * 512:(g + 1) * 512],
                            start=(fc == 0), stop=(fc == 7)),
                            wait=(wv_w if (fc == 0 and g == 0) else None),
                            inc=((pe, 1) if (fc == 7 and g == 1) else None))
                mmv = pe.n
                g_slot = t % 4
                self.act(lambda h, g_slot=g_slot, bA=bA: h.copy(
                    out=vR[:, g_slot, :], in_=self.ps[:, bA * 512:(bA + 2) * 512]),
                    wait=[(pe, mmv)], inc=(ac, 1))
                kv_evacs.append(ac.n)
            prev_kv_evac = ac.n

            gq_w = [(wl_g, wl_g_t), (dv, xn1T_done[t1]), (ac, prev_kv_evac)]
            for oc in range(4):
                bslot = (4 + (oc % 2), 256 * ((oc // 2) % 2))
                for fc in range(8):
                    self.pe(lambda h, oc=oc, fc=fc, b=bslot: h.matmul(
                        self.bank(b[0], b[1], b[1] + 256), wgq[:, fc, oc * 128:(oc + 1) * 128],
                        xn1c[:, fc, :], start=(fc == 0), stop=(fc == 7)),
                        wait=(gq_w if (oc == 0 and fc == 0) else None),
                        inc=((pe, 1) if fc == 7 else None))
                mmgq = pe.n
                self.act(lambda h, oc=oc, b=bslot: h.activation(
                    out=GQTc[:, oc, :], in_=self.bank(b[0], b[1], b[1] + 256),
                    func=AF.Copy, scale=0.125, bias=0.0),
                    wait=[(pe, mmgq)], inc=(ac, 1))
            gq_evac = ac.n

            for ti, t in enumerate((t0, t1)):
                g = t
                var = 1 if g == 0 else (2 if g == 1 else 0)
                mask = self.masksW[:].rearrange("p (v j) -> p v j", j=384)
                pvA = 2 * ti
                for hp in range(8):
                    hA, hB = 2 * hp, 2 * hp + 1
                    sb_ = (4, 5)
                    wsc = [(ac, gq_evac), (ac, exp_done_prev)]
                    if c == 0 and hp == 0:
                        wsc.append((po, ring_z))
                    first = True
                    for kb in range(3):
                        gk_ = g - 2 + kb
                        slot = gk_ % 4
                        for hh, (b_, prt) in enumerate((( sb_[0], 0), (sb_[1], 64))):
                            last = (kb == 2 and prt == 64)
                            self.pe(lambda h, hp=hp, kb=kb, ti=ti, b_=b_, prt=prt, slot=slot: h.matmul(
                                self.bank(b_, kb * 128, (kb + 1) * 128)[:, :],
                                qTc[prt:prt + 64, hp, ti * 128:(ti + 1) * 128],
                                kT[prt:prt + 64, hp, slot, :],
                                start=True, stop=True, tile_position=(prt, 0)),
                                wait=(wsc if first else None),
                                inc=((pe, 1) if last else None))
                            first = False
                    mm_sc = pe.n
                    for hh, b_ in enumerate(sb_):
                        self.dve(lambda h, b_=b_, var=var, mask=mask: h.tensor_tensor(
                            out=self.bank(b_, 0, 384), in0=self.bank(b_, 0, 384),
                            in1=mask[:, var, :], op=ALU.add),
                            wait=[(pe, mm_sc)] if hh == 0 else None, inc=(dv, 1))
                        madd = dv.n
                        self.dve(lambda h, b_=b_, hh=hh: h.tensor_reduce(
                            out=CL[:, 8 + hh:9 + hh], in_=self.bank(b_, 0, 384),
                            axis=AX.X, op=ALU.max, negate=True),
                            wait=[(dv, madd)], inc=(dv, 1))
                        mx = dv.n
                        self.act(lambda h, b_=b_, hh=hh: h.activation(
                            out=Pbuf[:, hh, :], in_=self.bank(b_, 0, 384), func=AF.Exp,
                            bias=CL[:, 8 + hh:9 + hh], accum_out=CL[:, 10 + hh:11 + hh]),
                            wait=[(dv, mx)], inc=(ac, 1))
                    exp_done = ac.n
                    exp_done_prev = exp_done
                    self.dve(lambda h: h.reciprocal(
                        out=CL[:, 12:14], in_=CL[:, 10:12]),
                        wait=[(ac, exp_done)], inc=(dv, 1))
                    rc = dv.n
                    for hh in range(2):
                        self.dve(lambda h, hh=hh: h.tensor_scalar(
                            out=Pbuf[:, hh, :], in0=Pbuf[:, hh, :],
                            scalar1=CL[:, 12 + hh:13 + hh], scalar2=None, op0=ALU.mult),
                            wait=([(dv, rc)] if hh == 0 else None), inc=((dv, 1) if hh == 1 else None))
                    pn = dv.n
                    wpt = [(dv, pn), (ac, ptevac_prev)]
                    for hh in range(2):
                        for kb in range(3):
                            self.pe(lambda h, hh=hh, kb=kb: h.transpose(
                                self.bankb(6, (hh * 3 + kb) * 128, (hh * 3 + kb + 1) * 128),
                                Pbuf[:, hh, kb * 128:(kb + 1) * 128], iB[:, :]),
                                wait=(wpt if (hh == 0 and kb == 0) else None),
                                inc=((pe, 1) if (hh == 1 and kb == 2) else None))
                    ptm = pe.n
                    self.act(lambda h: h.copy(
                        out=PT[:, :, :, :].rearrange("p h b j -> p (h b j)"),
                        in_=self.bankb(6, 0, 768)),
                        wait=[(pe, ptm)], inc=(ac, 1))
                    ptevac = ac.n
                    ptevac_prev = ptevac
                    for hh, hx in enumerate((hA, hB)):
                        bpv = pvA + (hx // 8)
                        col = (hx % 8) * 64
                        for kb in range(3):
                            gk_ = g - 2 + kb
                            slot = gk_ % 4
                            self.pe(lambda h, hh=hh, kb=kb, hx=hx, bpv=bpv, col=col, slot=slot: h.matmul(
                                self.bank(bpv, col, col + 64), PT[:, hh, kb, :],
                                vR[:, slot, hx * 64:(hx + 1) * 64],
                                start=(kb == 0), stop=(kb == 2), skip_group_check=True),
                                wait=([(ac, ptevac)] if (hh == 0 and kb == 0) else None),
                                inc=((pe, 1) if (hh == 1 and kb == 2) else None))
                pv_mm = pe.n
                self.act(lambda h, ti=ti, pvA=pvA: h.copy(
                    out=lcl[:, ti, :], in_=self.ps[:, pvA * 512:(pvA + 2) * 512]),
                    wait=[(pe, pv_mm)] + ([(self.s_st, self.s_st.n)] if c > 0 else []), inc=(ac, 1))
                pv_evac[2 * c + ti] = ac.n
                self.sp(lambda h, g=g, ti=ti: h.dma_start(
                    out=self.local_scr[g * 128:(g + 1) * 128, :], in_=lcl[:, ti, :]),
                    wait=[(ac, pv_evac[2 * c + ti])], inc=(self.s_st, 16))
                lcl_st[ti] = self.s_st.n

                lmask = self.masksL[:].rearrange("p (t n) -> p t n", n=64)
                for lp in range(4):
                    sbl = (4, 5)
                    wsl = [(ac, exp_done_prev)]
                    for hh, prt in enumerate((0, 64)):
                        self.pe(lambda h, lp=lp, ti=ti, hh=hh, prt=prt: h.matmul(
                            self.bank(sbl[hh], 0, 64), GQTc[prt:prt + 64, lp, ti * 128:(ti + 1) * 128],
                            GK[prt:prt + 64, lp, :], start=True, stop=True,
                            tile_position=(prt, 0)),
                            wait=(wsl if hh == 0 else None), inc=((pe, 1) if hh == 1 else None))
                    mm_ls = pe.n
                    for hh, b_ in enumerate(sbl):
                        self.dve(lambda h, b_=b_, g=g: h.tensor_tensor(
                            out=self.bank(b_, 0, 64), in0=self.bank(b_, 0, 64),
                            in1=lmask[:, g, :], op=ALU.add),
                            wait=[(pe, mm_ls)] if hh == 0 else None, inc=(dv, 1))
                        madd = dv.n
                        self.dve(lambda h, b_=b_, hh=hh: h.tensor_reduce(
                            out=CL[:, 8 + hh:9 + hh], in_=self.bank(b_, 0, 64),
                            axis=AX.X, op=ALU.max, negate=True),
                            wait=[(dv, madd)], inc=(dv, 1))
                        mx = dv.n
                        self.act(lambda h, b_=b_, hh=hh: h.activation(
                            out=Pbuf[:, hh, 0:64], in_=self.bank(b_, 0, 64), func=AF.Exp,
                            bias=CL[:, 8 + hh:9 + hh], accum_out=CL[:, 10 + hh:11 + hh]),
                            wait=[(dv, mx)], inc=(ac, 1))
                    exp_done = ac.n
                    exp_done_prev = exp_done
                    self.dve(lambda h: h.reciprocal(out=CL[:, 12:14], in_=CL[:, 10:12]),
                             wait=[(ac, exp_done)], inc=(dv, 1))
                    rc = dv.n
                    for hh in range(2):
                        self.dve(lambda h, hh=hh: h.tensor_scalar(
                            out=Pbuf[:, hh, 0:64], in0=Pbuf[:, hh, 0:64],
                            scalar1=CL[:, 12 + hh:13 + hh], scalar2=None, op0=ALU.mult),
                            wait=([(dv, rc)] if hh == 0 else None), inc=((dv, 1) if hh == 1 else None))
                    pn = dv.n
                    wpt = [(dv, pn), (ac, ptevac_prev)]
                    for hh in range(2):
                        self.pe(lambda h, hh=hh: h.transpose(
                            self.bankb(6, hh * 128, (hh + 1) * 128)[0:64, :],
                            Pbuf[:, hh, 0:64], iB[:, :]),
                            wait=(wpt if hh == 0 else None), inc=((pe, 1) if hh == 1 else None))
                    ptm = pe.n
                    self.act(lambda h: h.copy(
                        out=PT[0:64, 0, 0:2, :].rearrange("p h j -> p (h j)"),
                        in_=self.bankb(6, 0, 256)[0:64, :]),
                        wait=[(pe, ptm)], inc=(ac, 1))
                    ptevac = ac.n
                    ptevac_prev = ptevac
                    wrt = [(ac, ptevac)]
                    if lp == 0:
                        wrt.append((ac, pv_evac[2 * c + ti]))
                    for hh, hx in enumerate((2 * lp, 2 * lp + 1)):
                        self.pe(lambda h, hh=hh, hx=hx, lp=lp: h.matmul(
                            self.bank(2, lp * 128, lp * 128 + 128)[hh * 64:(hh + 1) * 64 + (64 if False else 0), :] if False else
                            self.ps[hh * 64:(hh + 1) * 64, 2 * 512 + lp * 128:2 * 512 + (lp + 1) * 128],
                            GV[0:64, hx * 64:(hx + 1) * 64], PT[0:64, 0, hh, :],
                            start=True, stop=True, tile_position=(0, hh * 64)),
                            wait=(wrt if hh == 0 else None), inc=((pe, 1) if hh == 1 else None))
                rt_mm = pe.n
                self.act(lambda h, ti=ti: h.copy(
                    out=remTc[:, :, ti * 128:(ti + 1) * 128],
                    in_=self.bank(2).rearrange("p (g j) -> p g j", j=128)),
                    wait=[(pe, rt_mm)] + ([(self.s_st, self.s_st.n)] if c > 0 else []), inc=(ac, 1))
                rt_evac = ac.n
                self.sp(lambda h, g=g, ti=ti: h.dma_start(
                    out=self.remT_scr[:, :, g * 128:(g + 1) * 128].rearrange("g p j -> p g j"),
                    in_=remTc[:, :, ti * 128:(ti + 1) * 128]),
                    wait=[(ac, rt_evac)], inc=(self.s_st, 16))
                rem_st[ti] = self.s_st.n
                pv_evac[2 * c + ti] = max(pv_evac[2 * c + ti], rt_evac)
            chunk_last_pe[c] = pe.n
        self._mla_m1_done = dict(pe=pe.n, dv=dv.n, ac=ac.n)
        self.dump("local_all", self.local_scr, [(self.s_st, self.s_st.n)])
        self.dump("remT_all", self.remT_scr, [(self.s_st, self.s_st.n)])
        self.dump("glgr", self.glgr[:, :], [(ac, ac.n)])

    def mla_m3(self, s):
        pe, dv, ac = self.s_pe, self.s_dv, self.s_ac
        iB, CL = self.identB, self.cols
        row = self.ar(0, BF16, 4 * 1024).rearrange("p (f o) -> p f o", o=1024)
        wout = self.ar(16384, BF16, 8 * 1024).rearrange("p (f o) -> p f o", o=1024)
        lclS = self.ar(107008, BF16, 2 * 1024).rearrange("p (s f) -> p s f", f=D)
        rmS = self.ar(90624, BF16, 2 * 4 * 128).rearrange("p (s g j) -> p s g j", g=4, j=128)
        outT = self.ar(90624 + 2048, BF16, 8 * 128).rearrange("p (f j) -> p f j", j=128)
        comb = self.ar(86528, F32, 1024)
        cmbf = self.ar(98816, BF16, 2 * 1024).rearrange("p (s f) -> p s f", f=D)
        xst = self.ar(111104, F32, 2 * 1024).rearrange("p (s f) -> p s f", f=D)
        gat = self.ar(65536, F32, 2 * 32).rearrange("p (s g) -> p s g", g=32)

        m1 = self._mla_m1_done
        wl = self.wl_next()
        self.pool(lambda h: h.dma_start(
            out=row[:, :, :], in_=self.w["r_out_w"].rearrange("(f p) o -> p f o", p=128)),
            wait=[(pe, m1["pe"])], inc=(wl, 16))
        self.pool(lambda h: h.dma_start(
            out=wout[:, :, :], in_=self.w["w_out"].rearrange("(f p) o -> p f o", p=128)),
            inc=(wl, 16))
        wl_t = wl.n

        done = {}
        prev_c1 = prev_lnd = prev_sqr = prev_otv = 0
        prev_tT = {}
        prev_store = {}
        for t in range(NT):
            sl = t % 2
            xl = self.s_xl[sl]
            w = [(dv, done.get(t - 2, 0))] + ([(self.s_st, self.s_st.n)] if t >= 2 else [])
            self.sp(lambda h, t=t, sl=sl: h.dma_start(
                out=lclS[:, sl, :], in_=self.local_scr[t * 128:(t + 1) * 128, :]),
                wait=w, inc=(xl, 16))
            self.sp(lambda h, t=t, sl=sl: h.dma_start(
                out=rmS[:, sl, :, :],
                in_=self.remT_scr[:, :, t * 128:(t + 1) * 128].rearrange("g p j -> p g j")),
                inc=(xl, 16))
            self.sp(lambda h, t=t, sl=sl: h.dma_start(
                out=xst[:, sl, :], in_=self.x_in[s, t * 128:(t + 1) * 128, :]),
                inc=(xl, 16))
            ld = xl.n
            wmm = [(xl, ld)] + ([(wl, wl_t)] if t == 0 else [])
            if t >= 1:
                wmm.append((dv, prev_c1))
            for rf in range(4):
                for g in range(2):
                    self.pe(lambda h, rf=rf, g=g, sl=sl: h.matmul(
                        self.bank(g), rmS[:, sl, rf, :], row[:, rf, g * 512:(g + 1) * 512],
                        start=(rf == 0), stop=(rf == 3)),
                        wait=(wmm if (rf == 0 and g == 0) else None),
                        inc=((pe, 1) if (rf == 3 and g == 1) else None))
            mmro = pe.n
            self.act(lambda h, t=t, sl=sl: h.activation(
                out=gat[:, sl, :], in_=self.glgr[:, t * 32:(t + 1) * 32], func=AF.Sigmoid),
                inc=(ac, 1))
            sig = ac.n
            glB = gat[:, sl, 0:16].unsqueeze(2).broadcast_to([128, 16, 64])
            grB = gat[:, sl, 16:32].unsqueeze(2).broadcast_to([128, 16, 64])
            self.pool(lambda h, sl=sl, glB=glB: h.tensor_tensor(
                out=cmbf[:, sl, :].rearrange("p (h d) -> p h d", d=64), in0=lclS[:, sl, :].rearrange("p (h d) -> p h d", d=64),
                in1=glB, op=ALU.mult),
                wait=[(ac, sig), (self.s_xl[sl], ld), (pe, prev_tT.get(sl, 0))], inc=(self.s_po, 1))
            pol = self.s_po.n
            self.dve(lambda h, grB=grB: h.tensor_tensor(
                out=comb[:, :].rearrange("p (h d) -> p h d", d=64),
                in0=self.ps[:, 0:1024].rearrange("p (h d) -> p h d", d=64), in1=grB, op=ALU.mult),
                wait=[(pe, mmro), (ac, sig)], inc=(dv, 1))
            c1 = dv.n
            prev_c1 = c1
            self.dve(lambda h, sl=sl: h.tensor_tensor(
                out=cmbf[:, sl, :], in0=comb[:, :], in1=cmbf[:, sl, :], op=ALU.add),
                wait=[(dv, c1), (self.s_po, pol)], inc=(dv, 1))
            cmb = dv.n
            for j in range(8):
                self.pe(lambda h, sl=sl, j=j: h.transpose(
                    self.bankb(6, j * 128, (j + 1) * 128), cmbf[:, sl, j * 128:(j + 1) * 128], iB[:, :]),
                    wait=([(dv, cmb), (ac, prev_otv)] if j == 0 else None), inc=((pe, 1) if j == 7 else None))
            tT = pe.n
            prev_tT[sl] = tT
            self.act(lambda h: h.copy(
                out=outT[:, :, :].rearrange("p f j -> p (f j)"), in_=self.bankb(6, 0, 1024)),
                wait=[(pe, tT)], inc=(ac, 1))
            otv = ac.n
            prev_otv = otv
            wwo = [(ac, otv)]
            if t >= 1:
                wwo += [(dv, prev_lnd), (ac, prev_sqr)]
            for fc in range(8):
                for g in range(2):
                    self.pe(lambda h, fc=fc, g=g: h.matmul(
                        self.bank(2 + g), outT[:, fc, :], wout[:, fc, g * 512:(g + 1) * 512],
                        start=(fc == 0), stop=(fc == 7)),
                        wait=(wwo if (fc == 0 and g == 0) else None),
                        inc=((pe, 1) if (fc == 7 and g == 1) else None))
            mmwo = pe.n
            ps23 = self.ps[:, 2 * 512:4 * 512]
            self.dve(lambda h, ps23=ps23: h.tensor_scalar(
                out=self.ar(65536 + 256, BF16, 1024)[:, :],
                in0=ps23, scalar1=-1.0 / D, scalar2=None,
                op0=ALU.mult, op1=ALU.add, accum_out=CL[:, 20:21]),
                wait=[(pe, mmwo)], inc=(dv, 1))
            nm = dv.n
            self.act(lambda h, ps23=ps23: h.activation(
                out=self.ar(65536 + 2304, BF16, 1024)[:, :], in_=ps23, func=AF.Square,
                bias=CL[:, 20:21], accum_out=CL[:, 21:22]),
                wait=[(dv, nm)], inc=(ac, 1))
            prev_sqr = ac.n
            self.act(lambda h: h.activation(
                out=CL[:, 22:23], in_=CL[:, 21:22], func=AF.Sqrt,
                scale=1.0 / D, bias=CL[:, 3:4]),
                wait=[(ac, ac.n)], inc=(ac, 1))
            sq = ac.n
            self.dve(lambda h: h.reciprocal(out=CL[:, 23:24], in_=CL[:, 22:23]),
                     wait=[(ac, sq)], inc=(dv, 1))
            rs = dv.n
            self.dve(lambda h, ps23=ps23: h.tensor_scalar(
                out=comb[:, :], in0=ps23, scalar1=CL[:, 20:21], scalar2=CL[:, 23:24],
                op0=ALU.add, op1=ALU.mult),
                wait=[(dv, rs)], inc=(dv, 1))
            lnd = dv.n
            prev_lnd = lnd
            self.dve(lambda h, sl=sl: h.tensor_tensor(
                out=xst[:, sl, :], in0=comb[:, :], in1=xst[:, sl, :], op=ALU.add),
                wait=[(dv, lnd)], inc=(dv, 1))
            x1d = dv.n
            self.sp(lambda h, t=t, sl=sl: h.dma_start(
                out=self.x1_scr[t * 128:(t + 1) * 128, :], in_=xst[:, sl, :]),
                wait=[(dv, x1d)], inc=(self.s_st, 16))
            prev_store[sl] = self.s_st.n
            self.dve(lambda h, sl=sl: h.tensor_scalar(
                out=self.ar(65536 + 256, BF16, 1024)[:, :], in0=xst[:, sl, :], scalar1=-1.0 / D,
                scalar2=None, op0=ALU.mult, op1=ALU.add, accum_out=CL[:, 24:25]),
                wait=[(dv, x1d)], inc=(dv, 1))
            nm2 = dv.n
            self.act(lambda h, sl=sl: h.activation(
                out=self.ar(65536 + 2304, BF16, 1024)[:, :], in_=xst[:, sl, :], func=AF.Square,
                bias=CL[:, 24:25], accum_out=CL[:, 25:26]),
                wait=[(dv, nm2)], inc=(ac, 1))
            self.act(lambda h: h.activation(
                out=CL[:, 26:27], in_=CL[:, 25:26], func=AF.Sqrt,
                scale=1.0 / D, bias=CL[:, 3:4]),
                wait=[(ac, ac.n)], inc=(ac, 1))
            sq2 = ac.n
            self.dve(lambda h: h.reciprocal(out=CL[:, 27:28], in_=CL[:, 26:27]),
                     wait=[(ac, sq2)], inc=(dv, 1))
            rs2 = dv.n
            self.dve(lambda h, sl=sl, t=t: h.tensor_scalar(
                out=self.ytile(t), in0=xst[:, sl, :], scalar1=CL[:, 24:25], scalar2=CL[:, 27:28],
                op0=ALU.add, op1=ALU.mult),
                wait=[(dv, rs2)], inc=(dv, 1))
            done[t] = dv.n
        self._m3_done = dict(dv=dv.n, ac=ac.n, pe=pe.n)
        self.dump("x1_all", self.x1_scr, [(self.s_st, self.s_st.n)])
        self.dump("y0", self.y[:, :], [(dv, dv.n)])

    def attn_layer(self, s, L):
        pe, dv, ac, po = self.s_pe, self.s_dv, self.s_ac, self.s_po
        iB, iF, CL = self.identB, self.identF, self.cols
        Wqkv = self.ar(0, BF16, 8 * 3072).rearrange("p (f o) -> p f o", o=3072)
        Wo = self.ar(49152, BF16, 8 * 1024).rearrange("p (f o) -> p f o", o=1024)
        TF = self.ar(65536, F32, 4 * 16 * 64).rearrange("p (a t d) -> p a t d", t=16, d=64)
        xsT = self.ar(81920, BF16, 8 * 129).rearrange("p (f j) -> p f j", j=129)
        qT = self.ar(84096, BF16, 8 * 128).rearrange("p (f j) -> p f j", j=128)
        kT = self.ar(86144, BF16, 8 * 4 * 128).rearrange("p (f g j) -> p f g j", g=4, j=128)
        vR = self.ar(94336, BF16, 4 * 1024).rearrange("p (g f) -> p g f", f=D)
        qb = self.ar(102528, BF16, 1024)
        sqb = self.ar(104576, BF16, 1024)
        outT = self.ar(106624, BF16, 8 * 128).rearrange("p (f j) -> p f j", j=128)
        scr = self.ar(108672, F32, 2 * 1024).rearrange("p (s f) -> p s f", f=D)
        PT = self.ar(116864, BF16, 2 * 3 * 128).rearrange("p (h b j) -> p h b j", b=3, j=128)
        Pbuf = self.ar(118400, BF16, 2 * 384).rearrange("p (h j) -> p h j", j=384)
        mask = self.masksW[:].rearrange("p (v j) -> p v j", j=384)

        wl = self.wl_next()
        self.pool(lambda h, L=L: h.dma_start(
            out=Wqkv[:, :, :], in_=self.w["Wqkv"][L].rearrange("(f p) o -> p f o", p=128)),
            inc=(wl, 16))
        self.pool(lambda h, L=L: h.dma_start(
            out=Wo[:, :, :], in_=self.w["Wo"][L].rearrange("(f p) o -> p f o", p=128)),
            inc=(wl, 16))
        wl_t = wl.n
        wl2 = self.sl_next()
        self.sp(lambda h, L=L: h.dma_start(
            out=self.qs_rep[:, :], in_=ap_view(self.w["qs"][L], 0, [[0, 128], [1, 64]])),
            inc=(wl2, 16))
        self.sp(lambda h, L=L: h.dma_start(
            out=self.ks_rep[:, :], in_=ap_view(self.w["ks"][L], 0, [[0, 128], [1, 64]])),
            inc=(wl2, 16))
        tabs = [("tab_Aq", 0, "q8"), ("tab_Bq", 1, "q8s"), ("tab_Ak", 2, "k"), ("tab_Bk", 3, "ks")]
        if not hasattr(self, "_tab_fold"):
            self._tab_fold = {}
        for name, idx, kind in tabs:
            self.sp(lambda h, name=name, idx=idx: h.dma_start(
                out=scr[:, idx % 2, :].rearrange("p (t d) -> p t d", d=64),
                in_=self.c[name].rearrange("(t p) d -> p t d", p=128)),
                wait=[(dv, self._tab_fold.get(idx - 1, 0))],
                inc=(wl2, 16))
            rep = self.qs_rep if kind.startswith("q") else self.ks_rep
            base = scr[:, idx % 2, :].rearrange("p (t d) -> p t d", d=64)
            halves = ([(slice(0, 32), slice(32, 64)), (slice(32, 64), slice(0, 32))]
                      if kind.endswith("s") else [(slice(0, 64), slice(0, 64))])
            for outsl, insl in halves:
                in1 = rep[:, insl].unsqueeze(1).broadcast_to([128, 16, insl.stop - insl.start])
                if kind.startswith("q"):
                    self.dve(lambda h, idx=idx, base=base, in1=in1, outsl=outsl: h.scalar_tensor_tensor(
                        out=TF[:, idx, :, outsl], in0=base[:, :, outsl], scalar=8.0, in1=in1,
                        op0=ALU.mult, op1=ALU.mult),
                        wait=[(wl2, wl2.n)], inc=(dv, 1))
                else:
                    self.dve(lambda h, idx=idx, base=base, in1=in1, outsl=outsl: h.tensor_tensor(
                        out=TF[:, idx, :, outsl], in0=base[:, :, outsl], in1=in1, op=ALU.mult),
                        wait=[(wl2, wl2.n)], inc=(dv, 1))
            self._tab_fold[idx] = dv.n
        tf_done = dv.n

        self.pool(lambda h: h.memset(kT[:, :, 2:4, :], 0.0))
        self.pool(lambda h: h.memset(vR[:, 2:4, :], 0.0), inc=(po, 1))
        ring_z = po.n

        st_ = dict(prev_qkv_mm=0, prev_xs_ev=("dv", 0), prev_ptevac=0, prev_exp=0,
                   prev_woadd=0, prev_otv=0, kv_ev={}, q_ev={}, pw={}, tf_done=tf_done,
                   ring_z=ring_z, wl_t=wl_t)

        def emit_load_and_qkv(t):
            slot = t % 4
            wcp = [(pe, st_["prev_qkv_mm"])]
            if t == 0:
                self.pool(lambda h: h.memset(xsT[:, :, 0:1], 0.0), wait=wcp, inc=(po, 1))
            else:
                self.pool(lambda h: h.tensor_copy(out=xsT[:, :, 0:1], in_=xsT[:, :, 128:129]),
                          wait=wcp, inc=(po, 1))
            carry = po.n
            e0, v0 = st_["prev_xs_ev"]
            wtr = [(self.s_po, carry), (ac, st_["prev_ptevac"]),
                   (dv if e0 == "dv" else ac, v0)]
            for j in range(8):
                bnk = 6 + (j // 4)
                self.pe(lambda h, t=t, j=j, bnk=bnk: h.transpose(
                    self.bank(bnk, (j % 4) * 128, (j % 4 + 1) * 128),
                    self.ytile(t)[:, j * 128:(j + 1) * 128], iF[:, :]),
                    wait=(wtr if j == 0 else None), inc=((pe, 1) if j == 7 else None))
            tmm = pe.n
            if t % 2 == 0:
                self.dve(lambda h: h.tensor_copy(
                    out=xsT[:, :, 1:129],
                    in_=self.ps[:, 6 * 512:8 * 512].rearrange("p (f j) -> p f j", j=128)),
                    wait=[(pe, tmm)], inc=(dv, 1))
                st_["prev_xs_ev"] = ("dv", dv.n)
            else:
                self.act(lambda h: h.copy(
                    out=xsT[:, :, 1:129],
                    in_=self.ps[:, 6 * 512:8 * 512].rearrange("p (f j) -> p f j", j=128)),
                    wait=[(pe, tmm)], inc=(ac, 1))
                st_["prev_xs_ev"] = ("ac", ac.n)
            e0, v0 = st_["prev_xs_ev"]
            wqk = [(dv if e0 == "dv" else ac, v0), (wl, st_["wl_t"])]
            if t >= 1:
                wqk += [(dv, st_["prev_woadd"]), (ac, st_["prev_otv"])]
            for g in range(6):
                for fc in range(8):
                    off = 1 if fc < 4 else 0
                    self.pe(lambda h, g=g, fc=fc, off=off: h.matmul(
                        self.bank(g), xsT[:, fc, off:off + 128],
                        Wqkv[:, fc, g * 512:(g + 1) * 512],
                        start=(fc == 0), stop=(fc == 7)),
                        wait=(wqk if (g == 0 and fc == 0) else None),
                        inc=((pe, 1) if fc == 7 else None))
            mm_all = pe.n
            st_["prev_qkv_mm"] = mm_all
            wv_ = [(pe, mm_all)]
            if t > 0:
                wv_.append((dv, st_["pw"].get(t - 1, 0)))
            self.dve(lambda h: h.tensor_copy(out=scr[:, 0, :], in_=self.ps[:, 0:1024]),
                     wait=wv_, inc=(dv, 1))
            q_ev = dv.n
            self.act(lambda h: h.copy(out=scr[:, 1, :], in_=self.ps[:, 1024:2048]),
                     wait=[(pe, mm_all)], inc=(ac, 1))
            k_ev = ac.n
            self.dve(lambda h, slot=slot: h.tensor_copy(
                out=vR[:, slot, :], in_=self.ps[:, 2048:3072]),
                wait=[(pe, mm_all)], inc=(dv, 1))
            v_ev = dv.n
            st_["kv_ev"][t] = (q_ev, k_ev, v_ev)

        def emit_pointwise(t):
            slot = t % 4
            q_ev, k_ev, v_ev = st_["kv_ev"][t]
            for which in (0, 1):
                src_ev = (dv, q_ev) if which == 0 else (ac, k_ev)
                self.act(lambda h, which=which: h.copy(out=qb[:, :], in_=scr[:, which, :]),
                         wait=[src_ev], inc=(ac, 1))
                cast_d = ac.n
                self.pool(lambda h: h.tensor_tensor(out=sqb[:, :], in0=qb[:, :], in1=qb[:, :], op=ALU.mult),
                          wait=[(ac, cast_d)], inc=(po, 1))
                sq_d = po.n
                self.dve(lambda h: h.tensor_reduce(
                    out=CL[:, 32:48], in_=sqb[:, :].rearrange("p (g d) -> p g d", d=64),
                    axis=AX.X, op=ALU.add),
                    wait=[(self.s_po, sq_d)], inc=(dv, 1))
                ssd = dv.n
                self.act(lambda h: h.activation(out=CL[:, 48:64], in_=CL[:, 32:48], func=AF.Sqrt),
                         wait=[(dv, ssd)], inc=(ac, 1))
                nsq = ac.n
                self.dve(lambda h: h.tensor_scalar(
                    out=CL[:, 32:48], in0=CL[:, 48:64], scalar1=1e-12, scalar2=None, op0=ALU.max),
                    wait=[(ac, nsq)], inc=(dv, 1))
                self.dve(lambda h: h.reciprocal(out=CL[:, 48:64], in_=CL[:, 32:48]),
                         wait=[(dv, dv.n)], inc=(dv, 1))
                rstd_d = dv.n
                rstd_bc = CL[:, 48:64].unsqueeze(2).broadcast_to([128, 16, 64])
                self.dve(lambda h, which=which, rstd_bc=rstd_bc: h.tensor_tensor(
                    out=scr[:, which, :].rearrange("p (g d) -> p g d", d=64),
                    in0=scr[:, which, :].rearrange("p (g d) -> p g d", d=64),
                    in1=rstd_bc, op=ALU.mult),
                    wait=[(dv, rstd_d)], inc=(dv, 1))
                q1_d = dv.n
                tbl = 1 if which == 0 else 3
                q3 = scr[:, which, :].rearrange("p (g d) -> p g d", d=64)
                s3 = sqb[:, :].rearrange("p (g d) -> p g d", d=64)
                for oi, (osl, isl) in enumerate((((0, 32), (32, 64)), ((32, 64), (0, 32)))):
                    FBh = TF[:, tbl, t, osl[0]:osl[1]].unsqueeze(1).broadcast_to([128, 16, 32])
                    self.pool(lambda h, q3=q3, s3=s3, FBh=FBh, osl=osl, isl=isl: h.tensor_tensor(
                        out=s3[:, :, osl[0]:osl[1]], in0=q3[:, :, isl[0]:isl[1]],
                        in1=FBh, op=ALU.mult),
                        wait=([(dv, q1_d)] if oi == 0 else None),
                        inc=((po, 1) if oi == 1 else None))
                rot_d = po.n
                FA = TF[:, 0 if which == 0 else 2, t, :].unsqueeze(1).broadcast_to([128, 16, 64])
                wfa = [(dv, q1_d), (self.s_po, rot_d)]
                if L == 0 and t == 0 and which == 0 and s == 0:
                    wfa.append((dv, st_["tf_done"]))
                self.dve(lambda h, which=which, FA=FA: h.tensor_tensor(
                    out=scr[:, which, :].rearrange("p (g d) -> p g d", d=64),
                    in0=scr[:, which, :].rearrange("p (g d) -> p g d", d=64),
                    in1=FA, op=ALU.mult),
                    wait=wfa, inc=(dv, 1))
                fa_d = dv.n
                self.dve(lambda h, which=which: h.tensor_tensor(
                    out=qb[:, :], in0=scr[:, which, :], in1=sqb[:, :], op=ALU.add),
                    wait=[(dv, fa_d), (self.s_po, rot_d)], inc=(dv, 1))
                qf_d = dv.n
                wtr2 = [(dv, qf_d), (ac, st_["prev_ptevac"])]
                for j in range(8):
                    self.pe(lambda h, j=j: h.transpose(
                        self.bankb(6, j * 128, (j + 1) * 128), qb[:, j * 128:(j + 1) * 128], iB[:, :]),
                        wait=(wtr2 if j == 0 else None), inc=((pe, 1) if j == 7 else None))
                tq = pe.n
                if which == 0:
                    self.act(lambda h: h.copy(
                        out=qT[:, :, :],
                        in_=self.bankb(6, 0, 1024).rearrange("p (f j) -> p f j", j=128)),
                        wait=[(pe, tq)], inc=(ac, 1))
                else:
                    self.act(lambda h, slot=slot: h.copy(
                        out=kT[:, :, slot, :],
                        in_=self.bankb(6, 0, 1024).rearrange("p (f j) -> p f j", j=128)),
                        wait=[(pe, tq)], inc=(ac, 1))
                st_["prev_ptevac"] = ac.n
            st_["pw"][t] = dv.n

        def emit_combos(t):
            var = 1 if t == 0 else (2 if t == 1 else 0)
            q_ev, k_ev, v_ev = st_["kv_ev"][t + 1] if (t + 1) in st_["kv_ev"] else (0, 0, 0)
            exp_at = {}
            for hp in range(8):
                sb0 = 4 if hp % 2 == 0 else 2
                wsc = [(ac, st_["prev_ptevac"])]
                if t == 0 and hp == 0:
                    wsc.append((self.s_po, st_["ring_z"]))
                if hp == 0:
                    wsc += [(dv, st_["prev_woadd"]), (ac, st_["prev_exp"])]
                    if (t + 1) in st_["kv_ev"]:
                        wsc += [(dv, st_["kv_ev"][t + 1][2]), (ac, st_["kv_ev"][t + 1][1])]
                if hp >= 2:
                    wsc.append((ac, exp_at[hp - 2]))
                first = True
                for kb in range(3):
                    kslot = (t - 2 + kb) % 4
                    for hh, prt in enumerate((0, 64)):
                        last = (kb == 2 and hh == 1)
                        self.pe(lambda h, hp=hp, kb=kb, hh=hh, prt=prt, kslot=kslot, sb0=sb0: h.matmul(
                            self.bank(sb0 + hh, kb * 128, (kb + 1) * 128),
                            qT[prt:prt + 64, hp, :], kT[prt:prt + 64, hp, kslot, :],
                            start=True, stop=True, tile_position=(prt, 0)),
                            wait=(wsc if first else None), inc=((pe, 1) if last else None))
                        first = False
                mm_sc = pe.n
                for hh in range(2):
                    self.dve(lambda h, hh=hh, var=var, sb0=sb0: h.tensor_tensor(
                        out=self.bank(sb0 + hh, 0, 384), in0=self.bank(sb0 + hh, 0, 384),
                        in1=mask[:, 3 + var, :], op=ALU.add),
                        wait=([(pe, mm_sc)] if hh == 0 else None), inc=(dv, 1))
                    madd = dv.n
                    self.dve(lambda h, hh=hh, sb0=sb0: h.tensor_reduce(
                        out=CL[:, 8 + hh:9 + hh], in_=self.bank(sb0 + hh, 0, 384),
                        axis=AX.X, op=ALU.max, negate=True),
                        wait=[(dv, madd)], inc=(dv, 1))
                    mx = dv.n
                    self.act(lambda h, hh=hh, sb0=sb0: h.activation(
                        out=Pbuf[:, hh, :], in_=self.bank(sb0 + hh, 0, 384), func=AF.Exp,
                        bias=CL[:, 8 + hh:9 + hh], accum_out=CL[:, 10 + hh:11 + hh]),
                        wait=[(dv, mx)], inc=(ac, 1))
                exp_at[hp] = ac.n
                st_["prev_exp"] = ac.n
                self.dve(lambda h: h.reciprocal(out=CL[:, 12:14], in_=CL[:, 10:12]),
                         wait=[(ac, st_["prev_exp"])], inc=(dv, 1))
                rc = dv.n
                for hh in range(2):
                    self.dve(lambda h, hh=hh: h.tensor_scalar(
                        out=Pbuf[:, hh, :], in0=Pbuf[:, hh, :],
                        scalar1=CL[:, 12 + hh:13 + hh], scalar2=None, op0=ALU.mult),
                        wait=([(dv, rc)] if hh == 0 else None), inc=((dv, 1) if hh == 1 else None))
                pn = dv.n
                e0, v0 = st_["prev_xs_ev"]
                wpt = [(dv, pn), (ac, st_["prev_ptevac"]),
                       (dv if e0 == "dv" else ac, v0)]
                for hh in range(2):
                    for kb in range(3):
                        self.pe(lambda h, hh=hh, kb=kb: h.transpose(
                            self.bankb(6, (hh * 3 + kb) * 128, (hh * 3 + kb + 1) * 128),
                            Pbuf[:, hh, kb * 128:(kb + 1) * 128], iB[:, :]),
                            wait=(wpt if (hh == 0 and kb == 0) else None),
                            inc=((pe, 1) if (hh == 1 and kb == 2) else None))
                ptm = pe.n
                self.act(lambda h: h.copy(
                    out=PT[:, :, :, :].rearrange("p h b j -> p (h b j)"),
                    in_=self.bankb(6, 0, 768)),
                    wait=[(pe, ptm)], inc=(ac, 1))
                st_["prev_ptevac"] = ac.n
                for hh in range(2):
                    for kb in range(3):
                        kslot = (t - 2 + kb) % 4
                        hx = 2 * hp + hh
                        last = (kb == 2 and hh == 1)
                        self.pe(lambda h, hh=hh, kb=kb, hx=hx, hp=hp, kslot=kslot: h.matmul(
                            self.ps[hh * 64:(hh + 1) * 64,
                                    (hp // 4) * 512 + (hp % 4) * 128:(hp // 4) * 512 + (hp % 4) * 128 + 128],
                            vR[:, kslot, hx * 64:(hx + 1) * 64], PT[:, hh, kb, :],
                            start=(kb == 0), stop=(kb == 2),
                            tile_position=(0, hh * 64), skip_group_check=True),
                            wait=([(ac, st_["prev_ptevac"])] if (kb == 0 and hh == 0) else None),
                            inc=((pe, 1) if last else None))
            pv_mm = pe.n
            for bb in range(2):
                self.act(lambda h, bb=bb: h.copy(
                    out=outT[:, 4 * bb:4 * bb + 4, :],
                    in_=self.bank(bb).rearrange("p (f j) -> p f j", j=128)),
                    wait=([(pe, pv_mm)] if bb == 0 else None), inc=((ac, 1) if bb == 1 else None))
            st_["prev_otv"] = ac.n
            wwo = [(ac, st_["prev_otv"]), (ac, st_["prev_exp"]), (dv, st_["prev_woadd"])]
            for fc in range(8):
                for gg in range(2):
                    self.pe(lambda h, fc=fc, gg=gg: h.matmul(
                        self.bank(4 + gg), outT[:, fc, :], Wo[:, fc, gg * 512:(gg + 1) * 512],
                        start=(fc == 0), stop=(fc == 7)),
                        wait=(wwo if (fc == 0 and gg == 0) else None),
                        inc=((pe, 1) if (fc == 7 and gg == 1) else None))
            mmwo = pe.n
            for gg in range(2):
                self.dve(lambda h, gg=gg, t=t: h.tensor_tensor(
                    out=self.ytile(t)[:, gg * 512:(gg + 1) * 512], in0=self.bank(4 + gg),
                    in1=self.ytile(t)[:, gg * 512:(gg + 1) * 512], op=ALU.add),
                    wait=([(pe, mmwo)] if gg == 0 else None), inc=((dv, 1) if gg == 1 else None))
            st_["prev_woadd"] = dv.n

        emit_load_and_qkv(0)
        emit_pointwise(0)
        for t in range(1, NT):
            emit_load_and_qkv(t)
            emit_combos(t - 1)
            emit_pointwise(t)
        emit_combos(NT - 1)
        self._attn_done = dict(pe=pe.n, dv=dv.n, ac=ac.n)

    def ffn_layer(self, s, L):
        pe, dv, ac, po = self.s_pe, self.s_dv, self.s_ac, self.s_po
        iB, CL = self.identB, self.cols
        xnT = self.ar(0, BF16, 8 * 2048).rearrange("p (f j) -> p f j", j=2048)
        W1c = [self.ar(32768 + i * 12480, BF16, 8 * 2 * 390).rearrange("p (f a j) -> p f a j", a=2, j=390) for i in range(2)]
        W2c = [self.ar(57728 + i * 8192, BF16, 4 * 1024).rearrange("p (s o) -> p s o", o=1024) for i in range(2)]
        hbuf = self.ar(74112, BF16, 4 * 2048).rearrange("p (s j) -> p s j", j=2048)
        sx = self.ar(90496, F32, 2 * 1024).rearrange("p (s f) -> p s f", f=D)
        g_rep = self.ar(98688, F32, 1024)
        b_rep = self.ar(102784, F32, 1024)
        b1c = self.ar(106880, F32, 56)
        xnf = self.ar(107104, F32, 1024)
        junkD = self.ar(111200, BF16, 1024)
        junkA = self.ar(113248, BF16, 1024)
        xnb = self.ar(115296, BF16, 1024)

        wl2 = self.sl_next()
        self.sp(lambda h, L=L: h.dma_start(
            out=g_rep[:, :], in_=ap_view(self.w["ln_g"][L], 0, [[0, 128], [1, D]])),
            inc=(wl2, 16))
        self.sp(lambda h, L=L: h.dma_start(
            out=b_rep[:, :], in_=ap_view(self.w["ln_b"][L], 0, [[0, 128], [1, D]])),
            inc=(wl2, 16))
        self.sp(lambda h, L=L: h.dma_start(out=b1c[:, :], in_=self.w["b1l"][L]),
            inc=(wl2, 16))
        wl2_t = wl2.n
        self.sp(lambda h: h.dma_start(out=hbuf[6:7, 3, :], in_=self.c["ones2048"]),
                inc=(self.s_cl, 16))
        ones_cl = self.s_cl.n

        def load_wchunk(cc, wsem, wait_pe):
            sl = cc % 2
            lo = cc * W_FFN
            for half in range(2):
                self.pool(lambda h, L=L, sl=sl, half=half, lo=lo: h.dma_start(
                    out=W1c[sl][:, :, half, :],
                    in_=self.w["W1"][L][:, half * DI + lo:half * DI + lo + W_FFN]
                        .rearrange("(f p) j -> p f j", p=128)),
                    wait=([(pe, wait_pe)] if half == 0 else None), inc=(wsem, 16))
            for sub in range(4):
                rlo = lo + sub * 128
                rhi = min(lo + W_FFN, rlo + 128)
                self.pool(lambda h, L=L, sl=sl, sub=sub, rlo=rlo, rhi=rhi: h.dma_start(
                    out=W2c[sl][0:rhi - rlo, sub, :], in_=self.w["W2"][L][rlo:rhi, :]),
                    inc=(wsem, 16))
            if cc == NFFC - 1:
                self.pool(lambda h, L=L, sl=sl: h.dma_start(
                    out=W2c[sl][6:7, 3, :], in_=self.w["b2"][L].unsqueeze(0)),
                    inc=(wsem, 16))

        wsems = [self.wl_next(), self.wl_next()]
        load_wchunk(0, wsems[0], 0)
        load_wchunk(1, wsems[1], 0)
        wtargets = [wsems[0].n, wsems[1].n]

        apply_done = {}
        xnT_ev = {}
        prev_tmm = 0
        for t in range(NT):
            sl = t % 2
            xl = self.s_xl[sl]
            w = [(dv, apply_done.get(t - 2, 0)), (self.s_po, po.n if t == 0 else 0)]
            self.sp(lambda h, t=t, sl=sl: h.dma_start(
                out=sx[:, sl, 0:512], in_=self.ytile(t)[:, 0:512]), wait=w, inc=(xl, 16))
            self.sp(lambda h, t=t, sl=sl: h.dma_start(
                out=sx[1:128, sl, 512:1024], in_=self.ytile(t)[0:127, 512:1024]), inc=(xl, 16))
            if t == 0:
                self.pool(lambda h, sl=sl: h.memset(sx[0:1, sl, 512:1024], 0.0), inc=(po, 1))
                bnd = (self.s_po, po.n)
            else:
                self.sp(lambda h, t=t, sl=sl: h.dma_start(
                    out=sx[0:1, sl, 512:1024], in_=self.ytile(t - 1)[127:128, 512:1024]),
                    inc=(xl, 16))
                bnd = (xl, xl.n)
            ld = xl.n
            self.dve(lambda h, sl=sl: h.tensor_scalar(
                out=junkD[:, :], in0=sx[:, sl, :], scalar1=-1.0 / D, scalar2=None,
                op0=ALU.mult, op1=ALU.add, accum_out=CL[:, 28:29]),
                wait=[(xl, ld), bnd], inc=(dv, 1))
            nm = dv.n
            self.act(lambda h, sl=sl: h.activation(
                out=junkA[:, :], in_=sx[:, sl, :], func=AF.Square,
                bias=CL[:, 28:29], accum_out=CL[:, 29:30]),
                wait=[(dv, nm)], inc=(ac, 1))
            self.act(lambda h: h.activation(
                out=CL[:, 30:31], in_=CL[:, 29:30], func=AF.Sqrt,
                scale=1.0 / D, bias=CL[:, 3:4]),
                wait=[(ac, ac.n)], inc=(ac, 1))
            sq = ac.n
            self.dve(lambda h: h.reciprocal(out=CL[:, 31:32], in_=CL[:, 30:31]),
                     wait=[(ac, sq)], inc=(dv, 1))
            rs = dv.n
            self.dve(lambda h, sl=sl: h.tensor_scalar(
                out=sx[:, sl, :], in0=sx[:, sl, :], scalar1=CL[:, 28:29], scalar2=CL[:, 31:32],
                op0=ALU.add, op1=ALU.mult),
                wait=[(dv, rs)], inc=(dv, 1))
            t1d = dv.n
            self.dve(lambda h, sl=sl: h.tensor_tensor(
                out=xnf[:, :], in0=sx[:, sl, :], in1=g_rep[:, :], op=ALU.mult),
                wait=[(dv, t1d), (wl2, wl2_t), (self.s_po, self._xnb_war.get(t - 1, 0) if hasattr(self, "_xnb_war") and (t - 1) in getattr(self, "_xnb_war", {}) else 0)],
                inc=(dv, 1))
            if not hasattr(self, "_xnb_war"):
                self._xnb_war = {}
            xg = dv.n
            self.pool(lambda h: h.tensor_tensor(
                out=xnb[:, :], in0=xnf[:, :], in1=b_rep[:, :], op=ALU.add),
                wait=[(dv, xg), (pe, prev_tmm)], inc=(po, 1))
            xnb_d = po.n
            apply_done[t] = dv.n
            wtr = [(self.s_po, xnb_d)]
            if t >= 1:
                wtr.append((dv if (t - 1) % 2 == 0 else ac, xnT_ev[t - 1]))
            for j in range(8):
                self.pe(lambda h, j=j: h.transpose(
                    self.bankb(6, j * 128, (j + 1) * 128), xnb[:, j * 128:(j + 1) * 128], iB[:, :]),
                    wait=(wtr if j == 0 else None), inc=((pe, 1) if j == 7 else None))
            tmm = pe.n
            prev_tmm = tmm
            self._xnb_war[t] = po.n
            if t % 2 == 0:
                self.dve(lambda h, t=t: h.tensor_copy(
                    out=xnT[:, :, t * 128:(t + 1) * 128],
                    in_=self.bankb(6, 0, 1024).rearrange("p (f j) -> p f j", j=128)),
                    wait=[(pe, tmm)], inc=(dv, 1))
                xnT_ev[t] = dv.n
            else:
                self.act(lambda h, t=t: h.copy(
                    out=xnT[:, :, t * 128:(t + 1) * 128],
                    in_=self.bankb(6, 0, 1024).rearrange("p (f j) -> p f j", j=128)),
                    wait=[(pe, tmm)], inc=(ac, 1))
                xnT_ev[t] = ac.n
        ln_all = (dv.n, ac.n)

        SUBW = [128, 128, 128, 6]
        h_written = {}
        w2_done = {}
        yadd_done = {}
        prev_gelu = 0
        prev_yadd = {}
        for cc in range(NFFC):
            sl = cc % 2
            wsem = wsems[sl]
            wtarget = wtargets[sl]
            if cc >= 2:
                pass
            ep_marks = []
            for tg in range(4):
                for sub in range(4):
                    ws = SUBW[sub]
                    ba = 0 + 2 * (len(ep_marks) % 2)
                    bg = 1 + 2 * (len(ep_marks) % 2)
                    wmm = [(wsem, wtarget)]
                    if sub == 0:
                        wmm.append((dv, xnT_ev.get(4 * tg + 2, 0)))
                        wmm.append((ac, xnT_ev.get(4 * tg + 3, 0)))
                        if tg == 0 and cc >= 1:
                            wmm.append((dv, self._ep_prev.get(cc - 1, 0)))
                    k = len(ep_marks)
                    if k >= 2:
                        wmm.append((dv, ep_marks[k - 2][1]))
                        wmm.append((ac, ep_marks[k - 2][0]))
                    for fc in range(8):
                        for half, bb in ((0, ba), (1, bg)):
                            self.pe(lambda h, fc=fc, half=half, bb=bb, sub=sub, tg=tg, ws=ws, sl=sl: h.matmul(
                                self.bank(bb)[0:ws, :], W1c[sl][:, fc, half, sub * 128:sub * 128 + ws],
                                xnT[:, fc, tg * 512:(tg + 1) * 512],
                                start=(fc == 0), stop=(fc == 7)),
                                wait=(wmm if (fc == 0 and half == 0) else None),
                                inc=((pe, 1) if (fc == 7 and half == 1) else None))
                    mmhg = pe.n
                    gcol = 28 + cc * 4 + sub
                    acol = cc * 4 + sub
                    gslot = (len(ep_marks) % 2) * 512
                    self.act(lambda h, bg=bg, ws=ws, gcol=gcol, gslot=gslot: h.activation(
                        out=xnf[0:ws, gslot:gslot + 512],
                        in_=self.bank(bg)[0:ws, :], func=AF.Gelu, bias=b1c[0:ws, gcol:gcol + 1]),
                        wait=[(pe, mmhg), (dv, prev_gelu)], inc=(ac, 1))
                    gel = ac.n
                    self.dve(lambda h, ba=ba, ws=ws, sub=sub, tg=tg, acol=acol, gslot=gslot: h.scalar_tensor_tensor(
                        out=hbuf[0:ws, sub, tg * 512:(tg + 1) * 512], in0=self.bank(ba)[0:ws, :],
                        scalar=b1c[0:ws, acol:acol + 1], in1=xnf[0:ws, gslot:gslot + 512],
                        op0=ALU.add, op1=ALU.mult),
                        wait=[(ac, gel), (pe, w2_done.get(cc - 1, 0))], inc=(dv, 1))
                    if k >= 1:
                        prev_gelu = ep_marks[k - 1][1] if k < 2 else ep_marks[k - 1][1]
                        prev_gelu = ep_marks[k - 1][1]
                    ep_marks.append((gel, dv.n))
                    if len(ep_marks) >= 2:
                        prev_gelu = ep_marks[len(ep_marks) - 2][1]
            self._ep_prev = getattr(self, "_ep_prev", {})
            self._ep_prev[cc] = dv.n
            h_done = dv.n
            for t in range(NT):
                byo = 4 + 2 * (t % 2)
                wm = [(dv, h_done)]
                if cc == NFFC - 1 and t == 0:
                    wm.append((self.s_cl, ones_cl))
                if t >= 2:
                    wm.append((dv, prev_yadd.get(t - 2, 0)))
                for og in range(2):
                    for sub in range(4):
                        ws = SUBW[sub] + (1 if (sub == 3 and cc == NFFC - 1) else 0)
                        self.pe(lambda h, t=t, og=og, sub=sub, ws=ws, sl=sl, byo=byo: h.matmul(
                            self.bank(byo + og), hbuf[0:ws, sub, t * 128:(t + 1) * 128],
                            W2c[sl][0:ws, sub, og * 512:(og + 1) * 512],
                            start=(sub == 0), stop=(sub == 3)),
                            wait=(wm if (og == 0 and sub == 0) else None),
                            inc=((pe, 1) if (og == 1 and sub == 3) else None))
                mmw2 = pe.n
                for og in range(2):
                    self.dve(lambda h, t=t, og=og, byo=byo: h.tensor_tensor(
                        out=self.ytile(t)[:, og * 512:(og + 1) * 512], in0=self.bank(byo + og),
                        in1=self.ytile(t)[:, og * 512:(og + 1) * 512], op=ALU.add),
                        wait=([(pe, mmw2)] if og == 0 else None), inc=((dv, 1) if og == 1 else None))
                prev_yadd[t] = dv.n
            w2_done[cc] = pe.n
            if cc + 2 < NFFC:
                load_wchunk(cc + 2, wsem, w2_done.get(cc, 0))
                wtargets[sl] = wsem.n
        self._ffn_done = dict(pe=pe.n, dv=dv.n, ac=ac.n)

    def final(self, s):
        pe, dv, ac = self.s_pe, self.s_dv, self.s_ac
        CL = self.cols
        xst = self.ar(111104, F32, 2 * 1024).rearrange("p (s f) -> p s f", f=D)
        ob = self.ar(86528, F32, 1024)
        junkD = self.ar(107008, BF16, 1024)
        junkA = self.ar(109056, BF16, 1024)
        done = {}
        st_prev = {}
        for t in range(NT):
            sl = t % 2
            xl = self.s_xl[sl]
            self.sp(lambda h, t=t, sl=sl: h.dma_start(
                out=xst[:, sl, :], in_=self.x1_scr[t * 128:(t + 1) * 128, :]),
                wait=[(dv, done.get(t - 2, 0))] + ([(self.s_st, self.s_st.n)] if t >= 2 else []), inc=(xl, 16))
            ld = xl.n
            self.dve(lambda h, t=t: h.tensor_scalar(
                out=junkD[:, :], in0=self.ytile(t), scalar1=-1.0 / D, scalar2=None,
                op0=ALU.mult, op1=ALU.add, accum_out=CL[:, 28:29]),
                wait=[(dv, done.get(t - 1, 0))], inc=(dv, 1))
            nm = dv.n
            self.act(lambda h, t=t: h.activation(
                out=junkA[:, :], in_=self.ytile(t), func=AF.Square,
                bias=CL[:, 28:29], accum_out=CL[:, 29:30]),
                wait=[(dv, nm)], inc=(ac, 1))
            self.act(lambda h: h.activation(
                out=CL[:, 30:31], in_=CL[:, 29:30], func=AF.Sqrt,
                scale=1.0 / D, bias=CL[:, 3:4]),
                wait=[(ac, ac.n)], inc=(ac, 1))
            sq = ac.n
            self.dve(lambda h: h.reciprocal(out=CL[:, 31:32], in_=CL[:, 30:31]),
                     wait=[(ac, sq)], inc=(dv, 1))
            rs = dv.n
            self.dve(lambda h, t=t: h.tensor_scalar(
                out=ob[:, :], in0=self.ytile(t), scalar1=CL[:, 28:29], scalar2=CL[:, 31:32],
                op0=ALU.add, op1=ALU.mult),
                wait=[(dv, rs)] + ([(self.s_st, self.s_st.n)] if t >= 1 else []), inc=(dv, 1))
            self.dve(lambda h, sl=sl: h.tensor_tensor(
                out=ob[:, :], in0=ob[:, :], in1=xst[:, sl, :], op=ALU.add),
                wait=[(dv, dv.n), (xl, ld)], inc=(dv, 1))
            done[t] = dv.n
            self.sp(lambda h, t=t, s=s: h.dma_start(
                out=self.out[s, t * 128:(t + 1) * 128, :], in_=ob[:, :]),
                wait=[(dv, done[t])], inc=(self.s_st, 16))
            st_prev[sl] = self.s_st.n
            st_prev["ob"] = self.s_st.n

    def build(self, phases=("all",)):
        self.load_consts()
        for s in range(2):
            self.phase0(s)
            self.barrier()
            if "p0" in phases:
                break
            self.mla(s)
            if "mla" in phases:
                self.barrier()
                break
            self.barrier()
            self.mla_m3(s)
            self.barrier()
            if "m3" in phases:
                break
            stop = False
            for L in range(NL):
                self.attn_layer(s, L)
                self.barrier()
                if f"attn{L}" in phases:
                    self.dump("y", self.y[:, :], [(self.s_dv, self.s_dv.n)])
                    stop = True
                    break
                self.ffn_layer(s, L)
                self.barrier()
                if f"ffn{L}" in phases:
                    self.dump("y", self.y[:, :], [(self.s_dv, self.s_dv.n)])
                    stop = True
                    break
            if stop:
                break
            self.final(s)
            self.barrier()
        self.sp(lambda h: h.wait_ge(self.s_st.h, self.s_st.n))
        self.build_block()


def build_nc(dbg=(), phases=("all",)):
    nc = bass.Bass("TRN2", target_bir_lowering=False, debug=False)
    mk = MK(nc, dbg)
    with mk.es:
        mk.build(phases=phases)
    return nc, mk


def make_in_maps(inputs):
    consts = host_consts()
    b1l = lay_b1(np.asarray(inputs["L0_ffn_b1"], dtype=np.float32))
    gw_bd = np.zeros((D, 32), dtype=np.float32)
    glw = np.asarray(inputs["L0_attn_gate_local_w"], dtype=np.float32)
    grw = np.asarray(inputs["L0_attn_gate_remote_w"], dtype=np.float32)
    for h in range(16):
        gw_bd[h * 64:(h + 1) * 64, h] = glw[h]
        gw_bd[h * 64:(h + 1) * 64, 16 + h] = grw[h]
    shared = {
        "wq": inputs["L0_attn_wq"], "wk": inputs["L0_attn_wk"], "wv": inputs["L0_attn_wv"],
        "wgq": inputs["L0_attn_wgq"], "wgk": inputs["L0_attn_wgk"], "wgv": inputs["L0_attn_wgv"],
        "r_out_w": inputs["L0_attn_r_out_w"], "glw": inputs["L0_attn_gate_local_w"],
        "grw": inputs["L0_attn_gate_remote_w"], "w_out": inputs["L0_attn_w_out"],
        "Wqkv": inputs["L0_ffn_Wqkv"], "qs": inputs["L0_ffn_q_scale"], "ks": inputs["L0_ffn_k_scale"],
        "Wo": inputs["L0_ffn_Wo"], "ln_g": inputs["L0_ffn_ln_g"], "ln_b": inputs["L0_ffn_ln_b"],
        "W1": inputs["L0_ffn_W1"], "b1l": b1l, "W2": inputs["L0_ffn_W2"], "b2": inputs["L0_ffn_b2"],
        "gw_bd": gw_bd,
    }
    shared = {k: np.ascontiguousarray(np.asarray(v, dtype=np.float32)) for k, v in shared.items()}
    shared.update(consts)
    x = np.ascontiguousarray(np.asarray(inputs["L0_attn_x"], dtype=np.float32))
    maps = []
    for c in range(8):
        m = dict(shared)
        m["x_in"] = x[2 * c:2 * c + 2]
        maps.append(m)
    return maps


def kernel(**inputs):
    nc, mk = build_nc()
    maps = make_in_maps(inputs)
    res = run_bass_kernel_spmd(nc, maps, core_ids=list(range(8)))
    out = np.empty((16, T, D), np.float32)
    for c in range(8):
        out[2 * c:2 * c + 2] = np.asarray(res.results[c]["out"])
    return out


# version 03
# speedup vs baseline: 1.1191x; 1.1191x over previous
import contextlib
import numpy as np
import ml_dtypes

import concourse.bass as bass
import concourse.mybir as mybir
import bass_rust
from concourse.bass_utils import run_bass_kernel_spmd

F32 = mybir.dt.float32
BF16 = mybir.dt.bfloat16
U8 = mybir.dt.uint8
AF = mybir.ActivationFunctionType
ALU = mybir.AluOpType
AX = mybir.AxisListType

T = 2048
NT = 16
NCH = 8
D = 1024
NFC = 8
NL = 6
DI = 2730
W_FFN = 390
NFFC = 7
MASKVAL = -1e9

DBG = {}
DBGC = 0



def host_consts():
    c = {}
    DH = 64
    WIN = 256
    half = np.arange(0, DH, 2, dtype=np.float64)
    inv_freq = 1.0 / 10000.0 ** (half / DH)
    t = np.arange(T, dtype=np.float64)
    freqs = t[:, None] * inv_freq[None, :]
    freqs = np.concatenate([freqs, freqs], -1)
    base_scale = (half + 0.4 * DH) / (1.4 * DH)
    power = (t - T // 2) / (WIN // 2)
    scale = base_scale[None, :] ** power[:, None]
    scale = np.concatenate([scale, scale], -1)
    cos = np.cos(freqs)
    sin = np.sin(freqs)
    sgn = np.where(np.arange(DH) < 32, -1.0, 1.0)[None, :]
    c["tab_Aq"] = (cos * scale).astype(np.float32)
    c["tab_Bq"] = (sgn * sin * scale).astype(np.float32)
    c["tab_Ak"] = (cos / scale).astype(np.float32)
    c["tab_Bk"] = (sgn * sin / scale).astype(np.float32)

    i = np.arange(128)[:, None]
    w = np.arange(384)[None, :]
    m = np.zeros((2, 3, 128, 384), dtype=np.float32)
    for kind in range(2):
        lo = i + (1 - kind)
        valid = (w >= lo) & (w <= i + 256)
        for var in range(3):
            vv = valid.copy()
            if var == 1:
                vv &= w >= 256
            if var == 2:
                vv &= w >= 128
            m[kind, var] = np.where(vv, 0.0, MASKVAL)
    c["masks_win"] = m.astype(ml_dtypes.bfloat16)

    p = np.arange(128)[:, None]
    n = np.arange(64)[None, :]
    lm = np.zeros((NT, 128, 64), dtype=np.float32)
    for tt in range(NT):
        lm[tt] = np.where(n <= (tt * 128 + p) // 32, 0.0, MASKVAL)
    c["masks_lat"] = lm.astype(ml_dtypes.bfloat16)

    pp = np.zeros((NT, 128, 64), dtype=np.float32)
    for tt in range(NT):
        pp[tt, np.arange(128), (tt * 128 + np.arange(128)) // 32] = 1.0 / 32.0
    c["ppool"] = pp.astype(ml_dtypes.bfloat16)

    c["ident_bf"] = np.eye(128, dtype=ml_dtypes.bfloat16)
    c["ones2048"] = np.ones((1, T), dtype=ml_dtypes.bfloat16)
    c["ident_f32"] = np.eye(128, dtype=np.float32)
    c["ones_row"] = np.ones((1, 128), dtype=np.float32)
    return c


def lay_b1(b1):
    out = np.zeros((NL, 128, 56), dtype=np.float32)
    for L in range(NL):
        col = 0
        for half in range(2):
            base = half * DI
            for ch in range(NFFC):
                for s in range(4):
                    lo = ch * W_FFN + s * 128
                    hi = min(ch * W_FFN + W_FFN, lo + 128)
                    if lo >= ch * W_FFN + W_FFN:
                        continue
                    nrow = hi - lo
                    out[L, :nrow, col] = b1[L, base + lo:base + hi]
                    col += 1
    return out



class Sem:
    def __init__(self, h, name):
        self.h = h
        self.name = name
        self.n = 0


class K:
    ENGINES = ("pe", "act", "dve", "sp", "pool")

    def __init__(self, nc):
        self.nc = nc
        self.q = {e: [] for e in self.ENGINES}
        self.sems = {}
        self.es = contextlib.ExitStack()
        self.barrier_sem = None
        self.nbarrier = 0

    def sem(self, name):
        s = Sem(self.es.enter_context(self.nc.semaphore(name)), name)
        self.sems[name] = s
        return s

    def emit(self, eng, fn, wait=None, inc=None):
        if wait:
            for s, v in wait:
                self.q[eng].append(("w", s, v))
        if inc:
            s, n = inc
            s.n += n
            self.q[eng].append(("i", fn, s, n))
        else:
            self.q[eng].append(("f", fn))

    def barrier(self, engines=None):
        self.nbarrier += 1
        sems = [self.s_pe, self.s_dv, self.s_ac, self.s_po, self.s_st,
                self.s_xl[0], self.s_xl[1]]
        for e in self.ENGINES:
            for s in sems:
                if s.n > 0:
                    self.q[e].append(("w", s, s.n))

    def build_block(self):
        nc = self.nc
        with nc.Block() as block:
            for eng, reg in (("pe", block.tensor), ("act", block.scalar),
                             ("dve", block.vector), ("sp", block.sync),
                             ("pool", block.gpsimd)):
                items = self.q[eng]
                def body(h, items=items):
                    for it in items:
                        if it[0] == "w":
                            h.wait_ge(it[1].h, it[2])
                        elif it[0] == "f":
                            it[1](h)
                        else:
                            instr = it[1](h)
                            instr.then_inc(it[2].h, it[3])
                reg(body)


def _nop(h):
    return h.nop()


def ap_view(t, offset_elems, ap_pairs):
    v = t.copy()
    v.ap = bass_rust.VecI64Pair(ap_pairs)
    v.offset = v.offset + offset_elems
    return v



class MK(K):
    def __init__(self, nc, dbg=()):
        super().__init__(nc)
        self.dbg_names = set(dbg)
        self.dbg_out = {}
        es = self.es

        dt = nc.dram_tensor
        self.x_in = dt("x_in", [2, T, D], F32, kind="ExternalInput").ap()
        self.out = dt("out", [2, T, D], F32, kind="ExternalOutput").ap()
        wnames = {
            "wq": [D, D], "wk": [D, D], "wv": [D, D],
            "wgq": [D, 512], "wgk": [D, 512], "wgv": [D, 512],
            "r_out_w": [512, D], "glw": [16, 64], "grw": [16, 64], "w_out": [D, D],
            "Wqkv": [NL, D, 3072], "qs": [NL, 64], "ks": [NL, 64],
            "Wo": [NL, D, D], "ln_g": [NL, D], "ln_b": [NL, D],
            "W1": [NL, D, 2 * DI], "b1l": [NL, 128, 56], "W2": [NL, DI, D],
            "b2": [NL, D], "gw_bd": [D, 32],
        }
        self.w = {k: dt(k, shp, F32, kind="ExternalInput").ap() for k, shp in wnames.items()}
        cshapes = {
            "tab_Aq": ([T, 64], F32), "tab_Bq": ([T, 64], F32),
            "tab_Ak": ([T, 64], F32), "tab_Bk": ([T, 64], F32),
            "masks_win": ([2, 3, 128, 384], BF16), "masks_lat": ([NT, 128, 64], BF16),
            "ppool": ([NT, 128, 64], BF16), "ident_bf": ([128, 128], BF16),
            "ident_f32": ([128, 128], F32), "ones_row": ([1, 128], F32),
            "ones2048": ([1, T], BF16),
        }
        self.c = {k: dt(k, shp, d, kind="ExternalInput").ap() for k, (shp, d) in cshapes.items()}
        self.x1_scr = dt("x1_scr", [T, D], F32, kind="Internal").ap()
        self.local_scr = dt("local_scr", [T, D], BF16, kind="Internal").ap()
        self.remT_scr = dt("remT_scr", [4, 128, T], BF16, kind="Internal").ap()

        sb = lambda name, shape, dtp: es.enter_context(nc.sbuf_tensor(name, shape, dtp))
        self.y = sb("y_res", [128, NT * D], F32)
        self.masksW = sb("masksW_sb", [128, 6 * 384], BF16)
        self.masksL = sb("masksL_sb", [128, NT * 64], BF16)
        self.ppool = sb("ppool_sb", [128, NT * 64], BF16)
        self.identB = sb("identB_sb", [128, 128], BF16)
        self.identF = sb("identF_sb", [128, 128], F32)
        self.ones_row = sb("ones_row_sb", [1, 128], F32)
        self.stats_mla = sb("stats_mla", [128, NT * 2], F32)
        self.qs_rep = sb("qs_rep", [128, 64], F32)
        self.ks_rep = sb("ks_rep", [128, 64], F32)
        self.glgr = sb("glgr", [128, NT * 32], F32)
        self.cols = sb("cols", [128, 64], F32)
        self.AR = 122368
        self.arena = sb("arena", [128, self.AR], U8)

        self.ps = es.enter_context(nc.psum_tensor("ps", [128, 8 * 512], F32))

        self.s_pe = self.sem("pe")
        self.s_dv = self.sem("dv")
        self.s_ac = self.sem("ac")
        self.s_po = self.sem("po")
        self.s_sp = self.sem("sp")
        self.s_wl = [self.sem(f"wl{i}") for i in range(6)]
        self._wl_i = 0
        self.s_sl = [self.sem(f"sl{i}") for i in range(3)]
        self._sl_i = 0
        self.s_xl = [self.sem("xl0"), self.sem("xl1")]
        self.s_st = self.sem("st")
        self.s_cl = self.sem("cl")

    def bank(self, b, lo=0, hi=512):
        return self.ps[:, b * 512 + lo:b * 512 + hi]

    def bankb(self, b, lo=0, hi=1024):
        return self.ps[:, b * 512:(b + 1) * 512].bitcast(BF16)[:, lo:hi]

    def ar(self, off, dtp, n):
        nb = n * (4 if dtp == F32 else 2 if dtp in (BF16,) else 1)
        assert off + nb <= self.AR, (off, nb)
        return self.arena[:, off:off + nb].bitcast(dtp)

    def ytile(self, t):
        return self.y[:, t * D:(t + 1) * D]

    def pe(self, fn, wait=None, inc=None):
        self.emit("pe", fn, wait, inc)

    def act(self, fn, wait=None, inc=None):
        self.emit("act", fn, wait, inc)

    def dve(self, fn, wait=None, inc=None):
        self.emit("dve", fn, wait, inc)

    def sp(self, fn, wait=None, inc=None):
        self.emit("sp", fn, wait, inc)

    def pool(self, fn, wait=None, inc=None):
        self.emit("pool", fn, wait, inc)

    def dump(self, name, view, wait):
        if name not in self.dbg_names:
            return
        shp = list(view.shape)
        d = self.nc.dram_tensor("dbg_" + name, shp, view.dtype, kind="ExternalOutput").ap()
        self.dbg_out[name] = shp
        self.sp(lambda h, d=d, v=view: h.dma_start(out=d, in_=v),
                wait=wait, inc=(self.s_st, 16))
        for e in self.ENGINES:
            self.q[e].append(("w", self.s_st, self.s_st.n))

    def wl_next(self):
        s = self.s_wl[self._wl_i % 6]
        self._wl_i += 1
        return s

    def sl_next(self):
        s = self.s_sl[self._sl_i % 3]
        self._sl_i += 1
        return s

    def load_consts(self):
        c, cl = self.c, self.s_cl
        loads = [
            (self.masksW[:].rearrange("p (a b n) -> p a b n", a=2, b=3), c["masks_win"].rearrange("a b p n -> p a b n")),
            (self.masksL[:].rearrange("p (t n) -> p t n", n=64), c["masks_lat"].rearrange("t p n -> p t n")),
            (self.ppool[:].rearrange("p (t n) -> p t n", n=64), c["ppool"].rearrange("t p n -> p t n")),
            (self.identB, c["ident_bf"]),
            (self.identF, c["ident_f32"]),
            (self.ones_row, c["ones_row"]),
        ]
        self.dve(lambda h: h.memset(self.cols[:, 3:4], 1e-5))
        for dst, src in loads:
            self.sp(lambda h, d=dst, s=src: h.dma_start(
                out=(d[:, :] if hasattr(d, "free_size") is False else d), in_=s), inc=(cl, 16))
        for e in self.ENGINES:
            self.q[e].append(("w", cl, cl.n))

    def phase0(self, s):
        nc = self.nc
        A_W1 = 49152
        wgk = self.ar(A_W1, BF16, 8 * 512).rearrange("p (f o) -> p f o", o=512)
        wgv = self.ar(A_W1 + 8192, BF16, 8 * 512).rearrange("p (f o) -> p f o", o=512)
        xst = self.ar(111104, F32, 2 * 1024).rearrange("p (s f) -> p s f", f=D)
        xn1 = self.ar(86528, BF16, 2 * 1024).rearrange("p (s f) -> p s f", f=D)
        junk = self.ar(107008, BF16, 1024)
        junk2 = self.ar(109056, BF16, 1024)
        xp_bf = self.ar(65536, BF16, 1024)
        xpT = self.ar(65536 + 2048, BF16, 8 * 64).rearrange("p (f n) -> p f n", n=64)
        GK = self.ar(65536 + 3072, BF16, 4 * 64).rearrange("p (g n) -> p g n", n=64)
        GV = self.ar(65536 + 3584, BF16, 512)
        self.p0 = dict(xp_bf=xp_bf, xpT=xpT, GK=GK, GV=GV)

        wl = self.wl_next()
        self.pool(lambda h: h.dma_start(out=wgk[:, :, :],
                  in_=self.w["wgk"].rearrange("(f p) o -> p f o", p=128)), inc=(wl, 16))
        self.pool(lambda h: h.dma_start(out=wgv[:, :, :],
                  in_=self.w["wgv"].rearrange("(f p) o -> p f o", p=128)), inc=(wl, 16))
        self._p0_wl_t = wl.n

        ps, iB = self.ps, self.identB
        pe, dv, ac = self.s_pe, self.s_dv, self.s_ac
        ST = self.stats_mla
        for t in range(NT):
            sl = t % 2
            xl = self.s_xl[sl]
            w = []
            if t >= 2:
                w = [(dv, self._dv_apply[t - 2]), (ac, self._ac_sq[t - 2])] if hasattr(self, "_dv_apply") else []
            self.sp(lambda h, t=t, sl=sl: h.dma_start(
                out=xst[:, sl, :], in_=self.x_in[s, t * 128:(t + 1) * 128, :]),
                wait=w, inc=(xl, 16))
            if t == 0:
                self._dv_apply = {}
                self._ac_sq = {}
            self.dve(lambda h, sl=sl, t=t: h.tensor_scalar(
                out=junk[:, :], in0=xst[:, sl, :], scalar1=-1.0 / D, scalar2=None,
                op0=ALU.mult, op1=ALU.add, accum_out=ST[:, 2 * t:2 * t + 1]),
                wait=[(xl, xl.n)], inc=(dv, 1))
            negmu_done = dv.n
            self.act(lambda h, sl=sl, t=t: h.activation(
                out=junk2[:, :], in_=xst[:, sl, :], func=AF.Square,
                bias=ST[:, 2 * t:2 * t + 1], accum_out=self.cols[:, 1:2]),
                wait=[(dv, negmu_done)], inc=(ac, 1))
            sq_acc = ac.n
            self.act(lambda h: h.activation(
                out=self.cols[:, 2:3], in_=self.cols[:, 1:2], func=AF.Sqrt,
                scale=1.0 / D, bias=self.cols[:, 3:4]),
                wait=[(ac, sq_acc)], inc=(ac, 1))
            self._ac_sq[t] = ac.n
            sq_done = ac.n
            self.dve(lambda h, t=t: h.reciprocal(
                out=ST[:, 2 * t + 1:2 * t + 2], in_=self.cols[:, 2:3]),
                wait=[(ac, sq_done)], inc=(dv, 1))
            rstd_done = dv.n
            w2 = [(pe, self._p0mm[t - 2])] if t >= 2 else []
            self.dve(lambda h, sl=sl, t=t: h.tensor_scalar(
                out=xn1[:, sl, :], in0=xst[:, sl, :],
                scalar1=ST[:, 2 * t:2 * t + 1], scalar2=ST[:, 2 * t + 1:2 * t + 2],
                op0=ALU.add, op1=ALU.mult),
                wait=w2 + [(dv, rstd_done)], inc=(dv, 1))
            self._dv_apply[t] = dv.n
            apl = dv.n
            if t == 0:
                self._p0mm = {}
            self.pe(lambda h, t=t, sl=sl: h.matmul(
                self.bank(6)[0:64, :], self.ppool[:, t * 64:(t + 1) * 64], xn1[:, sl, 0:512],
                start=(t == 0), stop=(t == NT - 1), skip_group_check=True),
                wait=[(dv, apl)] + ([(self.s_wl[0], 0)] if False else []))
            self.pe(lambda h, t=t, sl=sl: h.matmul(
                self.bank(7)[0:64, :], self.ppool[:, t * 64:(t + 1) * 64], xn1[:, sl, 512:1024],
                start=(t == 0), stop=(t == NT - 1), skip_group_check=True),
                inc=(pe, 1))
            self._p0mm[t] = pe.n

        pool_done = pe.n
        self.act(lambda h: h.copy(out=xp_bf[0:64, 0:512], in_=self.bank(6)[0:64, :]),
                 wait=[(ac, 0), (self.s_pe, pool_done)])
        self.act(lambda h: h.copy(out=xp_bf[0:64, 512:1024], in_=self.bank(7)[0:64, :]),
                 inc=(ac, 1))
        xp_done = ac.n
        self.dump("xp", xp_bf[0:64, :], [(ac, xp_done)])
        for j in range(8):
            self.pe(lambda h, j=j: h.transpose(
                self.bankb(0, j * 64, (j + 1) * 64)[0:128, :], xp_bf[0:64, j * 128:(j + 1) * 128], iB[0:64, 0:64]),
                wait=([(ac, xp_done)] if j == 0 else None),
                inc=((pe, 1) if j == 7 else None))
        xpT_t = pe.n
        self.dve(lambda h: h.tensor_copy(out=xpT[:, :, :], in_=self.bankb(0, 0, 512).rearrange("p (f n) -> p f n", n=64)),
                 wait=[(self.s_pe, xpT_t)], inc=(dv, 1))
        xpT_done = dv.n
        wlv = self.s_wl[(self._wl_i - 1) % 6]
        for oc in range(4):
            for fc in range(8):
                self.pe(lambda h, oc=oc, fc=fc: h.matmul(
                    self.bank(1, oc * 64, (oc + 1) * 64)[0:128, :].rearrange("p n -> p n"),
                    wgk[:, fc, oc * 128:(oc + 1) * 128], xpT[:, fc, :],
                    start=(fc == 0), stop=(fc == 7), skip_group_check=True),
                    wait=([(dv, xpT_done), (wlv, self._p0_wl_t)] if (oc == 0 and fc == 0) else None))
        for fc in range(8):
            self.pe(lambda h, fc=fc: h.matmul(
                self.bank(2)[0:64, :], xpT[:, fc, :], wgv[:, fc, :],
                start=(fc == 0), stop=(fc == 7)),
                inc=((pe, 1) if fc == 7 else None))
        gkv_done = pe.n
        self._gkv_done = gkv_done
        self.act(lambda h: h.copy(out=GK[:, :, :], in_=self.bank(1, 0, 256).rearrange("p (g n) -> p g n", n=64)),
                 wait=[(self.s_pe, gkv_done)])
        self.act(lambda h: h.copy(out=GV[0:64, :], in_=self.bank(2)[0:64, :]), inc=(ac, 1))
        self.dump("GK", GK[:, :, :], [(ac, ac.n)])
        self.dump("GV", GV[0:64, :], [(ac, ac.n)])
        self.dump("stats_mla", ST[:, :], [(dv, dv.n)])

    def mla(self, s):
        nc = self.nc
        pe, dv, ac, sp_, po = self.s_pe, self.s_dv, self.s_ac, self.s_po, self.s_sp
        ST, iB = self.stats_mla, self.identB
        wq = self.ar(0, BF16, 8 * 1024).rearrange("p (f o) -> p f o", o=1024)
        wk = self.ar(16384, BF16, 8 * 1024).rearrange("p (f o) -> p f o", o=1024)
        wv = self.ar(32768, BF16, 8 * 1024).rearrange("p (f o) -> p f o", o=1024)
        wgq = self.ar(49152, BF16, 8 * 512).rearrange("p (f o) -> p f o", o=512)
        gw_bd = self.ar(57344, BF16, 8 * 32).rearrange("p (f o) -> p f o", o=32)
        GQTc = self.ar(65536, BF16, 4 * 256).rearrange("p (g j) -> p g j", j=256)
        GK = self.p0["GK"]
        GV = self.p0["GV"]
        qTc = self.ar(65536 + 8704, BF16, 8 * 256).rearrange("p (f j) -> p f j", j=256)
        remTc = self.ar(65536 + 12800, BF16, 4 * 256).rearrange("p (g j) -> p g j", j=256)
        Pbuf = self.ar(65536 + 14848, BF16, 2 * 384).rearrange("p (h j) -> p h j", j=384)
        xn1c = self.ar(81920, BF16, 8 * 256).rearrange("p (f j) -> p f j", j=256)
        xtok = self.ar(86528, BF16, 2 * 1024).rearrange("p (s f) -> p s f", f=D)
        kT = self.ar(90624, BF16, 8 * 4 * 128).rearrange("p (f g j) -> p f g j", g=4, j=128)
        vR = self.ar(98816, BF16, 4 * 1024).rearrange("p (g f) -> p g f", f=D)
        lcl = self.ar(107008, BF16, 2 * 1024).rearrange("p (q f) -> p q f", f=D)
        xst = self.ar(111104, F32, 2 * 1024).rearrange("p (s f) -> p s f", f=D)
        PT = self.ar(119296, BF16, 2 * 3 * 128).rearrange("p (h b j) -> p h b j", b=3, j=128)

        CL = self.cols

        wl_qkv = self.wl_next()
        for dst, name in ((wq, "wq"), (wk, "wk"), (wv, "wv")):
            self.pool(lambda h, d=dst, n=name: h.dma_start(
                out=d[:, :, :], in_=self.w[n].rearrange("(f p) o -> p f o", p=128)),
                inc=(wl_qkv, 16))
        wl_qkv_t = wl_qkv.n
        wl_g = self.wl_next()
        self.pool(lambda h: h.dma_start(
            out=wgq[:, :, :], in_=self.w["wgq"].rearrange("(f p) o -> p f o", p=128)),
            wait=[(pe, self._gkv_done)], inc=(wl_g, 16))
        self.pool(lambda h: h.dma_start(
            out=gw_bd[:, :, :], in_=self.w["gw_bd"].rearrange("(f p) o -> p f o", p=128)),
            inc=(wl_g, 16))
        wl_g_t = wl_g.n

        self.pool(lambda h: h.memset(kT[:, :, 2:4, :], 0.0))
        self.pool(lambda h: h.memset(vR[:, 2:4, :], 0.0), inc=(po, 1))
        ring_z = po.n

        xn1T_done = {}
        q_evac = {}
        pv_evac = {}
        chunk_last_pe = {}
        apply_done = {}
        exp_done_prev = 0
        ptevac_prev = 0
        prev_kv_evac = 0
        mm_gl_done = {}
        lcl_st = {}
        rem_st = {}

        for c in range(NCH):
            t0, t1 = 2 * c, 2 * c + 1
            for t in (t0, t1):
                sl = t % 2
                xl = self.s_xl[sl]
                w = []
                if t >= 2:
                    w.append((dv, apply_done.get(t - 2, 0)))
                self.sp(lambda h, t=t, sl=sl: h.dma_start(
                    out=xst[:, sl, :], in_=self.x_in[s, t * 128:(t + 1) * 128, :]),
                    wait=w, inc=(xl, 16))
            for ti, t in enumerate((t0, t1)):
                sl = t % 2
                xl = self.s_xl[sl]
                w = [(xl, xl.n)]
                if c >= 1:
                    w.append((pe, self._qT_mm.get(t - 2, 0)))
                self.dve(lambda h, sl=sl, t=t: h.tensor_scalar(
                    out=xtok[:, sl, :], in0=xst[:, sl, :],
                    scalar1=ST[:, 2 * t:2 * t + 1], scalar2=ST[:, 2 * t + 1:2 * t + 2],
                    op0=ALU.add, op1=ALU.mult),
                    wait=w, inc=(dv, 1))
                apply_done[t] = dv.n
                bnk = 6 + ti
                wpe = [(dv, apply_done[t])]
                if c >= 1:
                    wpe.append((dv, xn1T_done.get(t - 2, 0)))
                    wpe.append((ac, ptevac_prev))
                for j in range(8):
                    self.pe(lambda h, sl=sl, j=j, bnk=bnk: h.transpose(
                        self.bankb(bnk, j * 128, (j + 1) * 128), xtok[:, sl, j * 128:(j + 1) * 128], iB[:, :]),
                        wait=(wpe if j == 0 else None), inc=((pe, 1) if j == 7 else None))
                tT = pe.n
                self.dve(lambda h, ti=ti, bnk=bnk: h.tensor_copy(
                    out=ap_view(xn1c[:, 0, 0:128], ti * 128, [[xn1c.tensor.shape[-1] if False else 0, 0]]) if False else
                    xn1c[:, :, ti * 128:(ti + 1) * 128],
                    in_=self.bankb(bnk, 0, 1024).rearrange("p (f j) -> p f j", j=128)),
                    wait=[(pe, tT)], inc=(dv, 1))
                xn1T_done[t] = dv.n

            if c == 0:
                self._qT_mm = {}
            wl_w = [(wl_qkv, wl_qkv_t)] if c == 0 else []
            kv_evacs = []
            for oc in range(8):
                bslot = (4 + (oc % 2), 256 * ((oc // 2) % 2))
                wk_w = list(wl_w) if oc == 0 else []
                wk_w.append((dv, xn1T_done[t1]))
                if oc >= 4:
                    wk_w.append((ac, kv_evacs[oc - 4]))
                for fc in range(8):
                    self.pe(lambda h, oc=oc, fc=fc, b=bslot: h.matmul(
                        self.bank(b[0], b[1], b[1] + 256), wk[:, fc, oc * 128:(oc + 1) * 128],
                        xn1c[:, fc, :], start=(fc == 0), stop=(fc == 7)),
                        wait=(wk_w if fc == 0 else None), inc=((pe, 1) if fc == 7 else None))
                mmk = pe.n
                s0 = (2 * c) % 4
                self.act(lambda h, oc=oc, b=bslot, s0=s0: h.copy(
                    out=kT[:, oc, s0:s0 + 2, :].rearrange("p g j -> p (g j)"),
                    in_=self.bank(b[0], b[1], b[1] + 256)),
                    wait=[(pe, mmk)], inc=(ac, 1))
                kv_evacs.append(ac.n)
            for ti, t in enumerate((t0, t1)):
                sl = t % 2
                bA = 2 * ti
                wq_w = [(dv, xn1T_done[t])]
                if c >= 1:
                    wq_w.append((ac, pv_evac.get(2 * (c - 1) + ti, 0)))
                for fc in range(8):
                    for g in range(2):
                        self.pe(lambda h, fc=fc, g=g, ti=ti, bA=bA: h.matmul(
                            self.bank(bA + g), xn1c[:, fc, ti * 128:(ti + 1) * 128],
                            wq[:, fc, g * 512:(g + 1) * 512],
                            start=(fc == 0), stop=(fc == 7)),
                            wait=(wq_w if (fc == 0 and g == 0) else None),
                            inc=((pe, 1) if (fc == 7 and g == 1) else None))
                mmq = pe.n
                self.act(lambda h, sl=sl, bA=bA: h.activation(
                    out=xtok[:, sl, :], in_=self.ps[:, bA * 512:(bA + 2) * 512],
                    func=AF.Copy, scale=0.125, bias=0.0),
                    wait=[(pe, mmq)], inc=(ac, 1))
                q_evac[t] = ac.n
                bnk = 6 + ti
                wq2 = [(ac, q_evac[t]), (dv, xn1T_done[t1])]
                for j in range(8):
                    self.pe(lambda h, sl=sl, j=j, bnk=bnk: h.transpose(
                        self.bankb(bnk, j * 128, (j + 1) * 128), xtok[:, sl, j * 128:(j + 1) * 128], iB[:, :]),
                        wait=(wq2 if j == 0 else None), inc=((pe, 1) if j == 7 else None))
                self._qT_mm[t] = pe.n
                tq = pe.n
                self.dve(lambda h, ti=ti, bnk=bnk: h.tensor_copy(
                    out=qTc[:, :, ti * 128:(ti + 1) * 128],
                    in_=self.bankb(bnk, 0, 1024).rearrange("p (f j) -> p f j", j=128)),
                    wait=[(pe, tq)], inc=(dv, 1))
                qT_done = dv.n
                glw_w = [(dv, qT_done)] + ([(wl_g, wl_g_t)] if (c == 0 and ti == 0) else [])
                for fc in range(8):
                    self.pe(lambda h, fc=fc, ti=ti, bA=bA: h.matmul(
                        self.bank(bA, 0, 32), qTc[:, fc, ti * 128:(ti + 1) * 128],
                        gw_bd[:, fc, :], start=(fc == 0), stop=(fc == 7)),
                        wait=(glw_w if fc == 0 else None), inc=((pe, 1) if fc == 7 else None))
                mm_gl = pe.n
                self.act(lambda h, t=t, bA=bA: h.activation(
                    out=self.glgr[:, t * 32:(t + 1) * 32], in_=self.bank(bA, 0, 32),
                    func=AF.Copy, scale=8.0, bias=0.0),
                    wait=[(pe, mm_gl)], inc=(ac, 1))
                mm_gl_done[t] = ac.n
                wv_w = [(ac, mm_gl_done[t])]
                for fc in range(8):
                    for g in range(2):
                        self.pe(lambda h, fc=fc, g=g, ti=ti, bA=bA: h.matmul(
                            self.bank(bA + g), xn1c[:, fc, ti * 128:(ti + 1) * 128],
                            wv[:, fc, g * 512:(g + 1) * 512],
                            start=(fc == 0), stop=(fc == 7)),
                            wait=(wv_w if (fc == 0 and g == 0) else None),
                            inc=((pe, 1) if (fc == 7 and g == 1) else None))
                mmv = pe.n
                g_slot = t % 4
                self.act(lambda h, g_slot=g_slot, bA=bA: h.copy(
                    out=vR[:, g_slot, :], in_=self.ps[:, bA * 512:(bA + 2) * 512]),
                    wait=[(pe, mmv)], inc=(ac, 1))
                kv_evacs.append(ac.n)
            prev_kv_evac = ac.n

            gq_w = [(wl_g, wl_g_t), (dv, xn1T_done[t1]), (ac, prev_kv_evac)]
            for oc in range(4):
                bslot = (4 + (oc % 2), 256 * ((oc // 2) % 2))
                for fc in range(8):
                    self.pe(lambda h, oc=oc, fc=fc, b=bslot: h.matmul(
                        self.bank(b[0], b[1], b[1] + 256), wgq[:, fc, oc * 128:(oc + 1) * 128],
                        xn1c[:, fc, :], start=(fc == 0), stop=(fc == 7)),
                        wait=(gq_w if (oc == 0 and fc == 0) else None),
                        inc=((pe, 1) if fc == 7 else None))
                mmgq = pe.n
                self.act(lambda h, oc=oc, b=bslot: h.activation(
                    out=GQTc[:, oc, :], in_=self.bank(b[0], b[1], b[1] + 256),
                    func=AF.Copy, scale=0.125, bias=0.0),
                    wait=[(pe, mmgq)], inc=(ac, 1))
            gq_evac = ac.n

            for ti, t in enumerate((t0, t1)):
                g = t
                var = 1 if g == 0 else (2 if g == 1 else 0)
                mask = self.masksW[:].rearrange("p (v j) -> p v j", j=384)
                pvA = 2 * ti
                for hp in range(8):
                    hA, hB = 2 * hp, 2 * hp + 1
                    sb_ = (4, 5)
                    wsc = [(ac, gq_evac), (ac, exp_done_prev)]
                    if c == 0 and hp == 0:
                        wsc.append((po, ring_z))
                    first = True
                    for kb in range(3):
                        gk_ = g - 2 + kb
                        slot = gk_ % 4
                        for hh, (b_, prt) in enumerate((( sb_[0], 0), (sb_[1], 64))):
                            last = (kb == 2 and prt == 64)
                            self.pe(lambda h, hp=hp, kb=kb, ti=ti, b_=b_, prt=prt, slot=slot: h.matmul(
                                self.bank(b_, kb * 128, (kb + 1) * 128)[:, :],
                                qTc[prt:prt + 64, hp, ti * 128:(ti + 1) * 128],
                                kT[prt:prt + 64, hp, slot, :],
                                start=True, stop=True, tile_position=(prt, 0)),
                                wait=(wsc if first else None),
                                inc=((pe, 1) if last else None))
                            first = False
                    mm_sc = pe.n
                    for hh, b_ in enumerate(sb_):
                        self.dve(lambda h, b_=b_, var=var, mask=mask: h.tensor_tensor(
                            out=self.bank(b_, 0, 384), in0=self.bank(b_, 0, 384),
                            in1=mask[:, var, :], op=ALU.add),
                            wait=[(pe, mm_sc)] if hh == 0 else None, inc=(dv, 1))
                        madd = dv.n
                        self.dve(lambda h, b_=b_, hh=hh: h.tensor_reduce(
                            out=CL[:, 8 + hh:9 + hh], in_=self.bank(b_, 0, 384),
                            axis=AX.X, op=ALU.max, negate=True),
                            wait=[(dv, madd)], inc=(dv, 1))
                        mx = dv.n
                        self.act(lambda h, b_=b_, hh=hh: h.activation(
                            out=Pbuf[:, hh, :], in_=self.bank(b_, 0, 384), func=AF.Exp,
                            bias=CL[:, 8 + hh:9 + hh], accum_out=CL[:, 10 + hh:11 + hh]),
                            wait=[(dv, mx)], inc=(ac, 1))
                    exp_done = ac.n
                    exp_done_prev = exp_done
                    self.dve(lambda h: h.reciprocal(
                        out=CL[:, 12:14], in_=CL[:, 10:12]),
                        wait=[(ac, exp_done)], inc=(dv, 1))
                    rc = dv.n
                    for hh in range(2):
                        self.dve(lambda h, hh=hh: h.tensor_scalar(
                            out=Pbuf[:, hh, :], in0=Pbuf[:, hh, :],
                            scalar1=CL[:, 12 + hh:13 + hh], scalar2=None, op0=ALU.mult),
                            wait=([(dv, rc)] if hh == 0 else None), inc=((dv, 1) if hh == 1 else None))
                    pn = dv.n
                    wpt = [(dv, pn), (ac, ptevac_prev)]
                    for hh in range(2):
                        for kb in range(3):
                            self.pe(lambda h, hh=hh, kb=kb: h.transpose(
                                self.bankb(6, (hh * 3 + kb) * 128, (hh * 3 + kb + 1) * 128),
                                Pbuf[:, hh, kb * 128:(kb + 1) * 128], iB[:, :]),
                                wait=(wpt if (hh == 0 and kb == 0) else None),
                                inc=((pe, 1) if (hh == 1 and kb == 2) else None))
                    ptm = pe.n
                    self.act(lambda h: h.copy(
                        out=PT[:, :, :, :].rearrange("p h b j -> p (h b j)"),
                        in_=self.bankb(6, 0, 768)),
                        wait=[(pe, ptm)], inc=(ac, 1))
                    ptevac = ac.n
                    ptevac_prev = ptevac
                    for hh, hx in enumerate((hA, hB)):
                        bpv = pvA + (hx // 8)
                        col = (hx % 8) * 64
                        for kb in range(3):
                            gk_ = g - 2 + kb
                            slot = gk_ % 4
                            self.pe(lambda h, hh=hh, kb=kb, hx=hx, bpv=bpv, col=col, slot=slot: h.matmul(
                                self.bank(bpv, col, col + 64), PT[:, hh, kb, :],
                                vR[:, slot, hx * 64:(hx + 1) * 64],
                                start=(kb == 0), stop=(kb == 2), skip_group_check=True),
                                wait=([(ac, ptevac)] if (hh == 0 and kb == 0) else None),
                                inc=((pe, 1) if (hh == 1 and kb == 2) else None))
                pv_mm = pe.n
                self.act(lambda h, ti=ti, pvA=pvA: h.copy(
                    out=lcl[:, ti, :], in_=self.ps[:, pvA * 512:(pvA + 2) * 512]),
                    wait=[(pe, pv_mm)] + ([(self.s_st, self.s_st.n)] if c > 0 else []), inc=(ac, 1))
                pv_evac[2 * c + ti] = ac.n
                self.sp(lambda h, g=g, ti=ti: h.dma_start(
                    out=self.local_scr[g * 128:(g + 1) * 128, :], in_=lcl[:, ti, :]),
                    wait=[(ac, pv_evac[2 * c + ti])], inc=(self.s_st, 16))
                lcl_st[ti] = self.s_st.n

                lmask = self.masksL[:].rearrange("p (t n) -> p t n", n=64)
                for lp in range(4):
                    sbl = (4, 5)
                    wsl = [(ac, exp_done_prev)]
                    for hh, prt in enumerate((0, 64)):
                        self.pe(lambda h, lp=lp, ti=ti, hh=hh, prt=prt: h.matmul(
                            self.bank(sbl[hh], 0, 64), GQTc[prt:prt + 64, lp, ti * 128:(ti + 1) * 128],
                            GK[prt:prt + 64, lp, :], start=True, stop=True,
                            tile_position=(prt, 0)),
                            wait=(wsl if hh == 0 else None), inc=((pe, 1) if hh == 1 else None))
                    mm_ls = pe.n
                    for hh, b_ in enumerate(sbl):
                        self.dve(lambda h, b_=b_, g=g: h.tensor_tensor(
                            out=self.bank(b_, 0, 64), in0=self.bank(b_, 0, 64),
                            in1=lmask[:, g, :], op=ALU.add),
                            wait=[(pe, mm_ls)] if hh == 0 else None, inc=(dv, 1))
                        madd = dv.n
                        self.dve(lambda h, b_=b_, hh=hh: h.tensor_reduce(
                            out=CL[:, 8 + hh:9 + hh], in_=self.bank(b_, 0, 64),
                            axis=AX.X, op=ALU.max, negate=True),
                            wait=[(dv, madd)], inc=(dv, 1))
                        mx = dv.n
                        self.act(lambda h, b_=b_, hh=hh: h.activation(
                            out=Pbuf[:, hh, 0:64], in_=self.bank(b_, 0, 64), func=AF.Exp,
                            bias=CL[:, 8 + hh:9 + hh], accum_out=CL[:, 10 + hh:11 + hh]),
                            wait=[(dv, mx)], inc=(ac, 1))
                    exp_done = ac.n
                    exp_done_prev = exp_done
                    self.dve(lambda h: h.reciprocal(out=CL[:, 12:14], in_=CL[:, 10:12]),
                             wait=[(ac, exp_done)], inc=(dv, 1))
                    rc = dv.n
                    for hh in range(2):
                        self.dve(lambda h, hh=hh: h.tensor_scalar(
                            out=Pbuf[:, hh, 0:64], in0=Pbuf[:, hh, 0:64],
                            scalar1=CL[:, 12 + hh:13 + hh], scalar2=None, op0=ALU.mult),
                            wait=([(dv, rc)] if hh == 0 else None), inc=((dv, 1) if hh == 1 else None))
                    pn = dv.n
                    wpt = [(dv, pn), (ac, ptevac_prev)]
                    for hh in range(2):
                        self.pe(lambda h, hh=hh: h.transpose(
                            self.bankb(6, hh * 128, (hh + 1) * 128)[0:64, :],
                            Pbuf[:, hh, 0:64], iB[:, :]),
                            wait=(wpt if hh == 0 else None), inc=((pe, 1) if hh == 1 else None))
                    ptm = pe.n
                    self.act(lambda h: h.copy(
                        out=PT[0:64, 0, 0:2, :].rearrange("p h j -> p (h j)"),
                        in_=self.bankb(6, 0, 256)[0:64, :]),
                        wait=[(pe, ptm)], inc=(ac, 1))
                    ptevac = ac.n
                    ptevac_prev = ptevac
                    wrt = [(ac, ptevac)]
                    if lp == 0:
                        wrt.append((ac, pv_evac[2 * c + ti]))
                    for hh, hx in enumerate((2 * lp, 2 * lp + 1)):
                        self.pe(lambda h, hh=hh, hx=hx, lp=lp: h.matmul(
                            self.bank(2, lp * 128, lp * 128 + 128)[hh * 64:(hh + 1) * 64 + (64 if False else 0), :] if False else
                            self.ps[hh * 64:(hh + 1) * 64, 2 * 512 + lp * 128:2 * 512 + (lp + 1) * 128],
                            GV[0:64, hx * 64:(hx + 1) * 64], PT[0:64, 0, hh, :],
                            start=True, stop=True, tile_position=(0, hh * 64)),
                            wait=(wrt if hh == 0 else None), inc=((pe, 1) if hh == 1 else None))
                rt_mm = pe.n
                self.act(lambda h, ti=ti: h.copy(
                    out=remTc[:, :, ti * 128:(ti + 1) * 128],
                    in_=self.bank(2).rearrange("p (g j) -> p g j", j=128)),
                    wait=[(pe, rt_mm)] + ([(self.s_st, self.s_st.n)] if c > 0 else []), inc=(ac, 1))
                rt_evac = ac.n
                self.sp(lambda h, g=g, ti=ti: h.dma_start(
                    out=self.remT_scr[:, :, g * 128:(g + 1) * 128].rearrange("g p j -> p g j"),
                    in_=remTc[:, :, ti * 128:(ti + 1) * 128]),
                    wait=[(ac, rt_evac)], inc=(self.s_st, 16))
                rem_st[ti] = self.s_st.n
                pv_evac[2 * c + ti] = max(pv_evac[2 * c + ti], rt_evac)
            chunk_last_pe[c] = pe.n
        self._mla_m1_done = dict(pe=pe.n, dv=dv.n, ac=ac.n)
        self.dump("local_all", self.local_scr, [(self.s_st, self.s_st.n)])
        self.dump("remT_all", self.remT_scr, [(self.s_st, self.s_st.n)])
        self.dump("glgr", self.glgr[:, :], [(ac, ac.n)])

    def mla_m3(self, s):
        pe, dv, ac = self.s_pe, self.s_dv, self.s_ac
        iB, CL = self.identB, self.cols
        row = self.ar(0, BF16, 4 * 1024).rearrange("p (f o) -> p f o", o=1024)
        wout = self.ar(16384, BF16, 8 * 1024).rearrange("p (f o) -> p f o", o=1024)
        lclS = self.ar(107008, BF16, 2 * 1024).rearrange("p (s f) -> p s f", f=D)
        rmS = self.ar(90624, BF16, 2 * 4 * 128).rearrange("p (s g j) -> p s g j", g=4, j=128)
        outT = self.ar(90624 + 2048, BF16, 8 * 128).rearrange("p (f j) -> p f j", j=128)
        comb = self.ar(86528, F32, 1024)
        cmbf = self.ar(98816, BF16, 2 * 1024).rearrange("p (s f) -> p s f", f=D)
        xst = self.ar(111104, F32, 2 * 1024).rearrange("p (s f) -> p s f", f=D)
        gat = self.ar(65536, F32, 2 * 32).rearrange("p (s g) -> p s g", g=32)

        m1 = self._mla_m1_done
        wl = self.wl_next()
        self.pool(lambda h: h.dma_start(
            out=row[:, :, :], in_=self.w["r_out_w"].rearrange("(f p) o -> p f o", p=128)),
            wait=[(pe, m1["pe"])], inc=(wl, 16))
        self.pool(lambda h: h.dma_start(
            out=wout[:, :, :], in_=self.w["w_out"].rearrange("(f p) o -> p f o", p=128)),
            inc=(wl, 16))
        wl_t = wl.n

        done = {}
        prev_c1 = prev_lnd = prev_sqr = prev_otv = 0
        prev_tT = {}
        prev_store = {}
        for t in range(NT):
            sl = t % 2
            xl = self.s_xl[sl]
            w = [(dv, done.get(t - 2, 0))] + ([(self.s_st, self.s_st.n)] if t >= 2 else [])
            self.sp(lambda h, t=t, sl=sl: h.dma_start(
                out=lclS[:, sl, :], in_=self.local_scr[t * 128:(t + 1) * 128, :]),
                wait=w, inc=(xl, 16))
            self.sp(lambda h, t=t, sl=sl: h.dma_start(
                out=rmS[:, sl, :, :],
                in_=self.remT_scr[:, :, t * 128:(t + 1) * 128].rearrange("g p j -> p g j")),
                inc=(xl, 16))
            self.sp(lambda h, t=t, sl=sl: h.dma_start(
                out=xst[:, sl, :], in_=self.x_in[s, t * 128:(t + 1) * 128, :]),
                inc=(xl, 16))
            ld = xl.n
            wmm = [(xl, ld)] + ([(wl, wl_t)] if t == 0 else [])
            if t >= 1:
                wmm.append((dv, prev_c1))
            for rf in range(4):
                for g in range(2):
                    self.pe(lambda h, rf=rf, g=g, sl=sl: h.matmul(
                        self.bank(g), rmS[:, sl, rf, :], row[:, rf, g * 512:(g + 1) * 512],
                        start=(rf == 0), stop=(rf == 3)),
                        wait=(wmm if (rf == 0 and g == 0) else None),
                        inc=((pe, 1) if (rf == 3 and g == 1) else None))
            mmro = pe.n
            self.act(lambda h, t=t, sl=sl: h.activation(
                out=gat[:, sl, :], in_=self.glgr[:, t * 32:(t + 1) * 32], func=AF.Sigmoid),
                inc=(ac, 1))
            sig = ac.n
            glB = gat[:, sl, 0:16].unsqueeze(2).broadcast_to([128, 16, 64])
            grB = gat[:, sl, 16:32].unsqueeze(2).broadcast_to([128, 16, 64])
            self.pool(lambda h, sl=sl, glB=glB: h.tensor_tensor(
                out=cmbf[:, sl, :].rearrange("p (h d) -> p h d", d=64), in0=lclS[:, sl, :].rearrange("p (h d) -> p h d", d=64),
                in1=glB, op=ALU.mult),
                wait=[(ac, sig), (self.s_xl[sl], ld), (pe, prev_tT.get(sl, 0))], inc=(self.s_po, 1))
            pol = self.s_po.n
            self.dve(lambda h, grB=grB: h.tensor_tensor(
                out=comb[:, :].rearrange("p (h d) -> p h d", d=64),
                in0=self.ps[:, 0:1024].rearrange("p (h d) -> p h d", d=64), in1=grB, op=ALU.mult),
                wait=[(pe, mmro), (ac, sig)], inc=(dv, 1))
            c1 = dv.n
            prev_c1 = c1
            self.dve(lambda h, sl=sl: h.tensor_tensor(
                out=cmbf[:, sl, :], in0=comb[:, :], in1=cmbf[:, sl, :], op=ALU.add),
                wait=[(dv, c1), (self.s_po, pol)], inc=(dv, 1))
            cmb = dv.n
            for j in range(8):
                self.pe(lambda h, sl=sl, j=j: h.transpose(
                    self.bankb(6, j * 128, (j + 1) * 128), cmbf[:, sl, j * 128:(j + 1) * 128], iB[:, :]),
                    wait=([(dv, cmb), (ac, prev_otv)] if j == 0 else None), inc=((pe, 1) if j == 7 else None))
            tT = pe.n
            prev_tT[sl] = tT
            self.act(lambda h: h.copy(
                out=outT[:, :, :].rearrange("p f j -> p (f j)"), in_=self.bankb(6, 0, 1024)),
                wait=[(pe, tT)], inc=(ac, 1))
            otv = ac.n
            prev_otv = otv
            wwo = [(ac, otv)]
            if t >= 1:
                wwo += [(dv, prev_lnd), (ac, prev_sqr)]
            for fc in range(8):
                for g in range(2):
                    self.pe(lambda h, fc=fc, g=g: h.matmul(
                        self.bank(2 + g), outT[:, fc, :], wout[:, fc, g * 512:(g + 1) * 512],
                        start=(fc == 0), stop=(fc == 7)),
                        wait=(wwo if (fc == 0 and g == 0) else None),
                        inc=((pe, 1) if (fc == 7 and g == 1) else None))
            mmwo = pe.n
            ps23 = self.ps[:, 2 * 512:4 * 512]
            self.dve(lambda h, ps23=ps23: h.tensor_scalar(
                out=self.ar(65536 + 256, BF16, 1024)[:, :],
                in0=ps23, scalar1=-1.0 / D, scalar2=None,
                op0=ALU.mult, op1=ALU.add, accum_out=CL[:, 20:21]),
                wait=[(pe, mmwo)], inc=(dv, 1))
            nm = dv.n
            self.act(lambda h, ps23=ps23: h.activation(
                out=self.ar(65536 + 2304, BF16, 1024)[:, :], in_=ps23, func=AF.Square,
                bias=CL[:, 20:21], accum_out=CL[:, 21:22]),
                wait=[(dv, nm)], inc=(ac, 1))
            prev_sqr = ac.n
            self.act(lambda h: h.activation(
                out=CL[:, 22:23], in_=CL[:, 21:22], func=AF.Sqrt,
                scale=1.0 / D, bias=CL[:, 3:4]),
                wait=[(ac, ac.n)], inc=(ac, 1))
            sq = ac.n
            self.dve(lambda h: h.reciprocal(out=CL[:, 23:24], in_=CL[:, 22:23]),
                     wait=[(ac, sq)], inc=(dv, 1))
            rs = dv.n
            self.dve(lambda h, ps23=ps23: h.tensor_scalar(
                out=comb[:, :], in0=ps23, scalar1=CL[:, 20:21], scalar2=CL[:, 23:24],
                op0=ALU.add, op1=ALU.mult),
                wait=[(dv, rs)], inc=(dv, 1))
            lnd = dv.n
            prev_lnd = lnd
            self.dve(lambda h, sl=sl: h.tensor_tensor(
                out=xst[:, sl, :], in0=comb[:, :], in1=xst[:, sl, :], op=ALU.add),
                wait=[(dv, lnd)], inc=(dv, 1))
            x1d = dv.n
            self.sp(lambda h, t=t, sl=sl: h.dma_start(
                out=self.x1_scr[t * 128:(t + 1) * 128, :], in_=xst[:, sl, :]),
                wait=[(dv, x1d)], inc=(self.s_st, 16))
            prev_store[sl] = self.s_st.n
            self.dve(lambda h, sl=sl: h.tensor_scalar(
                out=self.ar(65536 + 256, BF16, 1024)[:, :], in0=xst[:, sl, :], scalar1=-1.0 / D,
                scalar2=None, op0=ALU.mult, op1=ALU.add, accum_out=CL[:, 24:25]),
                wait=[(dv, x1d)], inc=(dv, 1))
            nm2 = dv.n
            self.act(lambda h, sl=sl: h.activation(
                out=self.ar(65536 + 2304, BF16, 1024)[:, :], in_=xst[:, sl, :], func=AF.Square,
                bias=CL[:, 24:25], accum_out=CL[:, 25:26]),
                wait=[(dv, nm2)], inc=(ac, 1))
            self.act(lambda h: h.activation(
                out=CL[:, 26:27], in_=CL[:, 25:26], func=AF.Sqrt,
                scale=1.0 / D, bias=CL[:, 3:4]),
                wait=[(ac, ac.n)], inc=(ac, 1))
            sq2 = ac.n
            self.dve(lambda h: h.reciprocal(out=CL[:, 27:28], in_=CL[:, 26:27]),
                     wait=[(ac, sq2)], inc=(dv, 1))
            rs2 = dv.n
            self.dve(lambda h, sl=sl, t=t: h.tensor_scalar(
                out=self.ytile(t), in0=xst[:, sl, :], scalar1=CL[:, 24:25], scalar2=CL[:, 27:28],
                op0=ALU.add, op1=ALU.mult),
                wait=[(dv, rs2)], inc=(dv, 1))
            done[t] = dv.n
        self._m3_done = dict(dv=dv.n, ac=ac.n, pe=pe.n)
        self.dump("x1_all", self.x1_scr, [(self.s_st, self.s_st.n)])
        self.dump("y0", self.y[:, :], [(dv, dv.n)])

    def attn_layer(self, s, L):
        pe, dv, ac, po = self.s_pe, self.s_dv, self.s_ac, self.s_po
        iB, iF, CL = self.identB, self.identF, self.cols
        Wqkv = self.ar(0, BF16, 8 * 3072).rearrange("p (f o) -> p f o", o=3072)
        Wo = self.ar(49152, BF16, 8 * 1024).rearrange("p (f o) -> p f o", o=1024)
        TF = self.ar(65536, F32, 4 * 16 * 64).rearrange("p (a t d) -> p a t d", t=16, d=64)
        xsT = self.ar(81920, BF16, 8 * 129).rearrange("p (f j) -> p f j", j=129)
        qT = self.ar(84096, BF16, 2 * 8 * 128).rearrange("p (r f j) -> p r f j", f=8, j=128)
        kT = self.ar(88192, BF16, 8 * 4 * 128).rearrange("p (f g j) -> p f g j", g=4, j=128)
        vR = self.ar(96384, BF16, 4 * 1024).rearrange("p (g f) -> p g f", f=D)
        sqb = self.ar(104576, BF16, 1024)
        outT = self.ar(106624, BF16, 8 * 128).rearrange("p (f j) -> p f j", j=128)
        scr = self.ar(108672, F32, 2 * 1024).rearrange("p (s f) -> p s f", f=D)
        PT = self.ar(116864, BF16, 2 * 3 * 128).rearrange("p (h b j) -> p h b j", b=3, j=128)
        Pbuf = self.ar(118400, BF16, 2 * 2 * 384).rearrange("p (r h j) -> p r h j", h=2, j=384)
        mask = self.masksW[:].rearrange("p (v j) -> p v j", j=384)

        wl = self.wl_next()
        self.pool(lambda h, L=L: h.dma_start(
            out=Wqkv[:, :, :], in_=self.w["Wqkv"][L].rearrange("(f p) o -> p f o", p=128)),
            inc=(wl, 16))
        self.pool(lambda h, L=L: h.dma_start(
            out=Wo[:, :, :], in_=self.w["Wo"][L].rearrange("(f p) o -> p f o", p=128)),
            inc=(wl, 16))
        wl_t = wl.n
        wl2 = self.sl_next()
        self.sp(lambda h, L=L: h.dma_start(
            out=self.qs_rep[:, :], in_=ap_view(self.w["qs"][L], 0, [[0, 128], [1, 64]])),
            inc=(wl2, 16))
        self.sp(lambda h, L=L: h.dma_start(
            out=self.ks_rep[:, :], in_=ap_view(self.w["ks"][L], 0, [[0, 128], [1, 64]])),
            inc=(wl2, 16))
        tabs = [("tab_Aq", 0, "q8"), ("tab_Bq", 1, "q8s"), ("tab_Ak", 2, "k"), ("tab_Bk", 3, "ks")]
        if not hasattr(self, "_tab_fold"):
            self._tab_fold = {}
        for name, idx, kind in tabs:
            self.sp(lambda h, name=name, idx=idx: h.dma_start(
                out=scr[:, idx % 2, :].rearrange("p (t d) -> p t d", d=64),
                in_=self.c[name].rearrange("(t p) d -> p t d", p=128)),
                wait=[(dv, self._tab_fold.get(idx - 1, 0))],
                inc=(wl2, 16))
            rep = self.qs_rep if kind.startswith("q") else self.ks_rep
            base = scr[:, idx % 2, :].rearrange("p (t d) -> p t d", d=64)
            halves = ([(slice(0, 32), slice(32, 64)), (slice(32, 64), slice(0, 32))]
                      if kind.endswith("s") else [(slice(0, 64), slice(0, 64))])
            for outsl, insl in halves:
                in1 = rep[:, insl].unsqueeze(1).broadcast_to([128, 16, insl.stop - insl.start])
                if kind.startswith("q"):
                    self.dve(lambda h, idx=idx, base=base, in1=in1, outsl=outsl: h.scalar_tensor_tensor(
                        out=TF[:, idx, :, outsl], in0=base[:, :, outsl], scalar=8.0, in1=in1,
                        op0=ALU.mult, op1=ALU.mult),
                        wait=[(wl2, wl2.n)], inc=(dv, 1))
                else:
                    self.dve(lambda h, idx=idx, base=base, in1=in1, outsl=outsl: h.tensor_tensor(
                        out=TF[:, idx, :, outsl], in0=base[:, :, outsl], in1=in1, op=ALU.mult),
                        wait=[(wl2, wl2.n)], inc=(dv, 1))
            self._tab_fold[idx] = dv.n
        tf_done = dv.n

        self.pool(lambda h: h.memset(kT[:, :, 2:4, :], 0.0))
        self.pool(lambda h: h.memset(vR[:, 2:4, :], 0.0), inc=(po, 1))
        ring_z = po.n

        st_ = dict(xsT_ev={}, prev_tr_ev=0, prev_qkv_mm=0, kv_ev={}, pw={},
                   prev_woadd=0, prev_otv=0, prev_exp=0, prev_ptevac=0,
                   wo_ready={}, qkv_emitted={})

        def emit_xsT(t):
            if t >= NT:
                return
            wcp = [(pe, st_["prev_qkv_mm"])]
            if t == 0:
                self.pool(lambda h: h.memset(xsT[:, :, 0:1], 0.0), wait=wcp, inc=(po, 1))
            else:
                self.pool(lambda h: h.tensor_copy(out=xsT[:, :, 0:1], in_=xsT[:, :, 128:129]),
                          wait=wcp, inc=(po, 1))
            carry = po.n
            wtr = [(self.s_po, carry)]
            wtr += st_.pop("_xsT_bank_wait", [])
            for j in range(8):
                bnk = j // 4
                self.pe(lambda h, t=t, j=j, bnk=bnk: h.transpose(
                    self.bank(bnk, (j % 4) * 128, (j % 4 + 1) * 128),
                    self.ytile(t)[:, j * 128:(j + 1) * 128], iF[:, :]),
                    wait=(wtr if j == 0 else None), inc=((pe, 1) if j == 7 else None))
            tmm = pe.n
            if t % 2 == 0:
                self.dve(lambda h: h.tensor_copy(
                    out=xsT[:, :, 1:129],
                    in_=self.ps[:, 0:1024].rearrange("p (f j) -> p f j", j=128)),
                    wait=[(pe, tmm)], inc=(dv, 1))
                st_["xsT_ev"][t] = ("dv", dv.n)
            else:
                self.act(lambda h: h.copy(
                    out=xsT[:, :, 1:129],
                    in_=self.ps[:, 0:1024].rearrange("p (f j) -> p f j", j=128)),
                    wait=[(pe, tmm)], inc=(ac, 1))
                st_["xsT_ev"][t] = ("ac", ac.n)

        def qkv_wave(t, wave, extra_wait):
            e0, v0 = st_["xsT_ev"][t]
            w = [(dv if e0 == "dv" else ac, v0), (wl, wl_t)] + extra_wait
            for gi in range(3):
                g = wave * 3 + gi
                for fc in range(8):
                    off = 1 if fc < 4 else 0
                    self.pe(lambda h, g=g, gi=gi, fc=fc, off=off: h.matmul(
                        self.bank(gi), xsT[:, fc, off:off + 128],
                        Wqkv[:, fc, g * 512:(g + 1) * 512],
                        start=(fc == 0), stop=(fc == 7)),
                        wait=(w if (gi == 0 and fc == 0) else None),
                        inc=((pe, 1) if fc == 7 else None))
            st_["prev_qkv_mm"] = pe.n
            return pe.n

        def qkv_evac(t, wave, mm, defer_v=False):
            slot = t % 4
            if wave == 0:
                wv_ = [(pe, mm)]
                if t > 0:
                    wv_.append((dv, st_["pw"].get(t - 1, 0)))
                self.dve(lambda h: h.tensor_copy(out=scr[:, 0, :], in_=self.ps[:, 0:1024]),
                         wait=wv_, inc=(dv, 1))
                self.act(lambda h: h.copy(out=scr[:, 1, 0:512], in_=self.bank(2)),
                         wait=[(pe, mm)], inc=(ac, 1))
                st_["kv_ev"][(t, 0)] = (dv.n, ac.n)
            else:
                self.act(lambda h: h.copy(out=scr[:, 1, 512:1024], in_=self.bank(0)),
                         wait=[(pe, mm)], inc=(ac, 1))
                if defer_v:
                    st_["pending_v"] = (t, mm)
                    st_["kv_ev"][(t, 1)] = (dv.n, ac.n)
                else:
                    self.dve(lambda h, slot=slot: h.tensor_copy(
                        out=vR[:, slot, :], in_=self.ps[:, 512:1536]),
                        wait=[(pe, mm)], inc=(dv, 1))
                    st_["kv_ev"][(t, 1)] = (dv.n, ac.n)

        def emit_pointwise(t):
            slot = t % 4
            qe = st_["kv_ev"][(t, 0)]
            ke = st_["kv_ev"][(t, 1)]
            last_qf = [0, 0]
            for which in (0, 1):
                src_ev = (dv, qe[0]) if which == 0 else (ac, ke[1])
                self.pool(lambda h, which=which: h.tensor_tensor(
                    out=sqb[:, :], in0=scr[:, which, :], in1=scr[:, which, :], op=ALU.mult),
                    wait=[src_ev, (dv, last_qf[0]), (pe, last_qf[1])], inc=(po, 1))
                sq_d = po.n
                self.dve(lambda h: h.tensor_reduce(
                    out=CL[:, 32:48], in_=sqb[:, :].rearrange("p (g d) -> p g d", d=64),
                    axis=AX.X, op=ALU.add),
                    wait=[(self.s_po, sq_d)], inc=(dv, 1))
                ssd = dv.n
                self.act(lambda h: h.activation(out=CL[:, 48:64], in_=CL[:, 32:48], func=AF.Sqrt),
                         wait=[(dv, ssd)], inc=(ac, 1))
                nsq = ac.n
                self.dve(lambda h: h.tensor_scalar(
                    out=CL[:, 32:48], in0=CL[:, 48:64], scalar1=1e-12, scalar2=None, op0=ALU.max),
                    wait=[(ac, nsq)], inc=(dv, 1))
                self.dve(lambda h: h.reciprocal(out=CL[:, 48:64], in_=CL[:, 32:48]),
                         wait=[(dv, dv.n)], inc=(dv, 1))
                rstd_d = dv.n
                rstd_bc = CL[:, 48:64].unsqueeze(2).broadcast_to([128, 16, 64])
                self.dve(lambda h, which=which, rstd_bc=rstd_bc: h.tensor_tensor(
                    out=scr[:, which, :].rearrange("p (g d) -> p g d", d=64),
                    in0=scr[:, which, :].rearrange("p (g d) -> p g d", d=64),
                    in1=rstd_bc, op=ALU.mult),
                    wait=[(dv, rstd_d)], inc=(dv, 1))
                q1_d = dv.n
                tbl = 1 if which == 0 else 3
                q3 = scr[:, which, :].rearrange("p (g d) -> p g d", d=64)
                s3 = sqb[:, :].rearrange("p (g d) -> p g d", d=64)
                for oi, (osl, isl) in enumerate((((0, 32), (32, 64)), ((32, 64), (0, 32)))):
                    FBh = TF[:, tbl, t, osl[0]:osl[1]].unsqueeze(1).broadcast_to([128, 16, 32])
                    self.pool(lambda h, q3=q3, s3=s3, FBh=FBh, osl=osl, isl=isl: h.tensor_tensor(
                        out=s3[:, :, osl[0]:osl[1]], in0=q3[:, :, isl[0]:isl[1]],
                        in1=FBh, op=ALU.mult),
                        wait=([(dv, q1_d)] if oi == 0 else None),
                        inc=((po, 1) if oi == 1 else None))
                rot_d = po.n
                FA = TF[:, 0 if which == 0 else 2, t, :].unsqueeze(1).broadcast_to([128, 16, 64])
                wfa = [(dv, q1_d), (self.s_po, rot_d)]
                if L == 0 and t == 0 and which == 0 and s == 0:
                    wfa.append((dv, tf_done))
                self.dve(lambda h, which=which, FA=FA: h.tensor_tensor(
                    out=scr[:, which, :].rearrange("p (g d) -> p g d", d=64),
                    in0=scr[:, which, :].rearrange("p (g d) -> p g d", d=64),
                    in1=FA, op=ALU.mult),
                    wait=wfa, inc=(dv, 1))
                fa_d = dv.n
                self.dve(lambda h, which=which: h.tensor_tensor(
                    out=sqb[:, :], in0=scr[:, which, :], in1=sqb[:, :], op=ALU.add),
                    wait=[(dv, fa_d), (self.s_po, rot_d)], inc=(dv, 1))
                qf_d = dv.n
                last_qf[0] = qf_d
                wtr2 = [(dv, qf_d), (ac, st_["prev_exp"]), (ac, st_["prev_ptevac"])]
                for j in range(8):
                    self.pe(lambda h, j=j: h.transpose(
                        self.bankb(7, j * 128, (j + 1) * 128), sqb[:, j * 128:(j + 1) * 128], iB[:, :]),
                        wait=(wtr2 if j == 0 else None), inc=((pe, 1) if j == 7 else None))
                tq = pe.n
                last_qf[1] = tq
                if which == 0:
                    self.act(lambda h, t=t: h.copy(
                        out=qT[:, t % 2, :, :],
                        in_=self.bankb(7, 0, 1024).rearrange("p (f j) -> p f j", j=128)),
                        wait=[(pe, tq)], inc=(ac, 1))
                else:
                    self.act(lambda h, slot=slot: h.copy(
                        out=kT[:, :, slot, :],
                        in_=self.bankb(7, 0, 1024).rearrange("p (f j) -> p f j", j=128)),
                        wait=[(pe, tq)], inc=(ac, 1))
                st_["prev_tr_ev"] = ac.n
            st_["pw"][t] = dv.n

        def emit_wo(t):
            wwo = [(ac, st_["wo_ready"][t]), (dv, st_["prev_woadd"]),
                   (ac, st_["prev_ptevac"]), (ac, st_["prev_exp"])]
            for fc in range(8):
                for gg in range(2):
                    self.pe(lambda h, fc=fc, gg=gg: h.matmul(
                        self.bank(4 + gg), outT[:, fc, :], Wo[:, fc, gg * 512:(gg + 1) * 512],
                        start=(fc == 0), stop=(fc == 7)),
                        wait=(wwo if (fc == 0 and gg == 0) else None),
                        inc=((pe, 1) if (fc == 7 and gg == 1) else None))
            mmwo = pe.n
            for gg in range(2):
                self.dve(lambda h, gg=gg, t=t: h.tensor_tensor(
                    out=self.ytile(t)[:, gg * 512:(gg + 1) * 512], in0=self.bank(4 + gg),
                    in1=self.ytile(t)[:, gg * 512:(gg + 1) * 512], op=ALU.add),
                    wait=([(pe, mmwo)] if gg == 0 else None), inc=((dv, 1) if gg == 1 else None))
            st_["prev_woadd"] = dv.n

        def emit_combos(t, fillers):
            var = 1 if t == 0 else (2 if t == 1 else 0)
            tr_ev_t = st_["prev_tr_ev"]
            exp_at = {}
            ptevac_at = {}
            pv_at = {}
            fi = [0]

            def fill():
                if fi[0] < len(fillers):
                    fillers[fi[0]]()
                    fi[0] += 1

            for hp in range(10):
                if hp < 8:
                    par = hp % 2
                    sb0 = 4 + 2 * par
                    wsc = [(ac, tr_ev_t if par == 0 else st_["prev_tr_ev"])]
                    if t == 0 and hp == 0:
                        wsc.append((self.s_po, ring_z))
                    if hp < 2:
                        wsc += [(ac, st_["prev_exp"]), (ac, st_["prev_ptevac"]),
                                (dv, st_["prev_woadd"])]
                    if hp >= 2:
                        wsc += [(ac, exp_at[hp - 2]), (ac, ptevac_at.get(hp - 2, 0))]
                    first = True
                    for kb in range(3):
                        kslot = (t - 2 + kb) % 4
                        for hh, prt in enumerate((0, 64)):
                            last = (kb == 2 and hh == 1)
                            self.pe(lambda h, hp=hp, kb=kb, hh=hh, prt=prt, kslot=kslot, sb0=sb0: h.matmul(
                                self.bank(sb0 + hh, kb * 128, (kb + 1) * 128),
                                qT[prt:prt + 64, t % 2, hp, :], kT[prt:prt + 64, hp, kslot, :],
                                start=True, stop=True, tile_position=(prt, 0)),
                                wait=(wsc if first else None), inc=((pe, 1) if last else None))
                            first = False
                    mm_sc = pe.n
                    spair = self.ps[:, sb0 * 512:(sb0 + 2) * 512].rearrange(
                        "p (h j) -> p h j", j=512)[:, :, 0:384]
                    cl0 = 8 if par == 0 else 16
                    self.dve(lambda h, spair=spair, var=var: h.tensor_tensor(
                        out=spair, in0=spair,
                        in1=mask[:, 3 + var, :].unsqueeze(1).broadcast_to([128, 2, 384]),
                        op=ALU.add),
                        wait=[(pe, mm_sc)], inc=(dv, 1))
                    madd = dv.n
                    self.dve(lambda h, spair=spair, cl0=cl0: h.tensor_reduce(
                        out=CL[:, cl0:cl0 + 2], in_=spair, axis=AX.X, op=ALU.max, negate=True),
                        wait=[(dv, madd)], inc=(dv, 1))
                    mx = dv.n
                    for hh in range(2):
                        self.act(lambda h, hh=hh, sb0=sb0, par=par, cl0=cl0: h.activation(
                            out=Pbuf[:, par, hh, :], in_=self.bank(sb0 + hh, 0, 384), func=AF.Exp,
                            bias=CL[:, cl0 + hh:cl0 + hh + 1], accum_out=CL[:, cl0 + 2 + hh:cl0 + 3 + hh]),
                            wait=([(dv, mx)] if hh == 0 else None), inc=((ac, 1) if hh == 1 else None))
                    exp_at[hp] = ac.n
                    st_["prev_exp"] = ac.n
                    self.dve(lambda h, cl0=cl0: h.reciprocal(out=CL[:, cl0 + 4:cl0 + 6], in_=CL[:, cl0 + 2:cl0 + 4]),
                             wait=[(ac, exp_at[hp])], inc=(dv, 1))
                    rc = dv.n
                    rbc = CL[:, cl0 + 4:cl0 + 6].unsqueeze(2).broadcast_to([128, 2, 384]) if False else None
                    self.dve(lambda h, par=par, cl0=cl0: h.tensor_tensor(
                        out=Pbuf[:, par, :, :], in0=Pbuf[:, par, :, :],
                        in1=CL[:, cl0 + 4:cl0 + 6].unsqueeze(2).broadcast_to([128, 2, 384]),
                        op=ALU.mult),
                        wait=[(dv, rc)], inc=(dv, 1))
                    exp_at[hp] = max(exp_at[hp], 0)
                    st_[f"pn_{hp}"] = dv.n
                fill()
                hq = hp - 1
                if 0 <= hq < 8:
                    parq = hq % 2
                    sbq = 4 + 2 * parq
                    pnq = st_[f"pn_{hq}"]
                    wpt = [(dv, pnq), (pe, pv_at.get(hq - 1, 0))]
                    wpt.append((ac, ptevac_at.get(hq - 1, 0)))
                    for hh in range(2):
                        for kb in range(3):
                            self.pe(lambda h, hh=hh, kb=kb, sbq=sbq, parq=parq: h.transpose(
                                self.bankb(sbq, (hh * 3 + kb) * 128, (hh * 3 + kb + 1) * 128),
                                Pbuf[:, parq, hh, kb * 128:(kb + 1) * 128], iB[:, :]),
                                wait=(wpt if (hh == 0 and kb == 0) else None),
                                inc=((pe, 1) if (hh == 1 and kb == 2) else None))
                    ptm = pe.n
                    self.act(lambda h, sbq=sbq: h.copy(
                        out=PT[:, :, :, :].rearrange("p h b j -> p (h b j)"),
                        in_=self.bankb(sbq, 0, 768)),
                        wait=[(pe, ptm)], inc=(ac, 1))
                    ptevac_at[hq] = ac.n
                    st_["prev_ptevac"] = ac.n
                    for hh in range(2):
                        for kb in range(3):
                            kslot = (t - 2 + kb) % 4
                            hx = 2 * hq + hh
                            last = (kb == 2 and hh == 1)
                            self.pe(lambda h, hh=hh, kb=kb, hx=hx, hq=hq, kslot=kslot: h.matmul(
                                self.ps[hh * 64:(hh + 1) * 64,
                                        3 * 512 + (hq % 4) * 128:3 * 512 + (hq % 4) * 128 + 128],
                                vR[:, kslot, hx * 64:(hx + 1) * 64], PT[:, hh, kb, :],
                                start=(kb == 0), stop=(kb == 2),
                                tile_position=(0, hh * 64), skip_group_check=True),
                                wait=([(ac, ptevac_at[hq])] if (kb == 0 and hh == 0) else None),
                                inc=((pe, 1) if last else None))
                    pv_at[hq] = pe.n
                    if hq == 3 or hq == 7:
                        bb = hq // 4
                        wev = [(pe, pv_at[hq])]
                        if bb == 1:
                            wev.append((pe, st_["prev_qkv_mm"]))
                        self.act(lambda h, bb=bb: h.copy(
                            out=outT[:, 4 * bb:4 * bb + 4, :],
                            in_=self.bank(3).rearrange("p (f j) -> p f j", j=128)),
                            wait=wev, inc=(ac, 1))
                        st_["wo_ready"][t] = ac.n
            st_["prev_otv"] = st_["wo_ready"][t]

        emit_xsT(0)
        m = qkv_wave(0, 0, [])
        qkv_evac(0, 0, m)
        m = qkv_wave(0, 1, [(dv, st_["kv_ev"][(0, 0)][0]), (ac, st_["kv_ev"][(0, 0)][1])])
        qkv_evac(0, 1, m)
        emit_pointwise(0)
        st_["_xsT_bank_wait"] = [(ac, st_["kv_ev"][(0, 1)][1]), (dv, st_["kv_ev"][(0, 1)][0])]
        emit_xsT(1)
        m = qkv_wave(1, 0, [])
        qkv_evac(1, 0, m)
        m = qkv_wave(1, 1, [(dv, st_["kv_ev"][(1, 0)][0]), (ac, st_["kv_ev"][(1, 0)][1])])
        qkv_evac(1, 1, m)

        for t in range(NT):
            fillers = []
            if t + 1 < NT:
                fillers.append(lambda t=t: emit_pointwise(t + 1))
            if t + 2 < NT:
                def fx(t=t):
                    st_["_xsT_bank_wait"] = [(ac, st_["kv_ev"][(t + 1, 1)][1]),
                                             (dv, st_["kv_ev"][(t + 1, 1)][0]),
                                             (dv, st_["prev_woadd"])]
                    emit_xsT(t + 2)
                fillers.append(fx)
                def fa(t=t):
                    m = qkv_wave(t + 2, 0, [(dv, st_["prev_woadd"])])
                    qkv_evac(t + 2, 0, m)
                fillers.append(fa)
                def fb(t=t):
                    m = qkv_wave(t + 2, 1, [(dv, st_["kv_ev"][(t + 2, 0)][0]),
                                            (ac, st_["kv_ev"][(t + 2, 0)][1])])
                    qkv_evac(t + 2, 1, m, defer_v=True)
                fillers.append(fb)
            emit_combos(t, fillers)
            if "pending_v" in st_:
                tv, mmv = st_.pop("pending_v")
                self.dve(lambda h, slot=tv % 4: h.tensor_copy(
                    out=vR[:, slot, :], in_=self.ps[:, 512:1536]),
                    wait=[(pe, mmv), (pe, pe.n)], inc=(dv, 1))
                st_["kv_ev"][(tv, 1)] = (dv.n, st_["kv_ev"][(tv, 1)][1])
            emit_wo(t)
        self._attn_done = dict(pe=pe.n, dv=dv.n, ac=ac.n)

    def ffn_layer(self, s, L):
        pe, dv, ac, po = self.s_pe, self.s_dv, self.s_ac, self.s_po
        iB, CL = self.identB, self.cols
        xnT = self.ar(0, BF16, 8 * 2048).rearrange("p (f j) -> p f j", j=2048)
        W1c = [self.ar(32768 + i * 12480, BF16, 8 * 2 * 390).rearrange("p (f a j) -> p f a j", a=2, j=390) for i in range(2)]
        W2c = [self.ar(57728 + i * 8192, BF16, 4 * 1024).rearrange("p (s o) -> p s o", o=1024) for i in range(2)]
        hbuf = self.ar(74112, BF16, 4 * 2048).rearrange("p (s j) -> p s j", j=2048)
        sx = self.ar(90496, F32, 2 * 1024).rearrange("p (s f) -> p s f", f=D)
        g_rep = self.ar(98688, F32, 1024)
        b_rep = self.ar(102784, F32, 1024)
        b1c = self.ar(106880, F32, 56)
        xnf = self.ar(107104, F32, 1024)
        junkD = self.ar(111200, BF16, 1024)
        junkA = self.ar(113248, BF16, 1024)
        xnb = self.ar(115296, BF16, 1024)

        wl2 = self.sl_next()
        self.sp(lambda h, L=L: h.dma_start(
            out=g_rep[:, :], in_=ap_view(self.w["ln_g"][L], 0, [[0, 128], [1, D]])),
            inc=(wl2, 16))
        self.sp(lambda h, L=L: h.dma_start(
            out=b_rep[:, :], in_=ap_view(self.w["ln_b"][L], 0, [[0, 128], [1, D]])),
            inc=(wl2, 16))
        self.sp(lambda h, L=L: h.dma_start(out=b1c[:, :], in_=self.w["b1l"][L]),
            inc=(wl2, 16))
        wl2_t = wl2.n
        self.sp(lambda h: h.dma_start(out=hbuf[6:7, 3, :], in_=self.c["ones2048"]),
                inc=(self.s_cl, 16))
        ones_cl = self.s_cl.n

        def load_wchunk(cc, wsem, wait_pe):
            sl = cc % 2
            lo = cc * W_FFN
            for half in range(2):
                self.pool(lambda h, L=L, sl=sl, half=half, lo=lo: h.dma_start(
                    out=W1c[sl][:, :, half, :],
                    in_=self.w["W1"][L][:, half * DI + lo:half * DI + lo + W_FFN]
                        .rearrange("(f p) j -> p f j", p=128)),
                    wait=([(pe, wait_pe)] if half == 0 else None), inc=(wsem, 16))
            for sub in range(4):
                rlo = lo + sub * 128
                rhi = min(lo + W_FFN, rlo + 128)
                self.pool(lambda h, L=L, sl=sl, sub=sub, rlo=rlo, rhi=rhi: h.dma_start(
                    out=W2c[sl][0:rhi - rlo, sub, :], in_=self.w["W2"][L][rlo:rhi, :]),
                    inc=(wsem, 16))
            if cc == NFFC - 1:
                self.pool(lambda h, L=L, sl=sl: h.dma_start(
                    out=W2c[sl][6:7, 3, :], in_=self.w["b2"][L].unsqueeze(0)),
                    inc=(wsem, 16))

        wsems = [self.wl_next(), self.wl_next()]
        load_wchunk(0, wsems[0], 0)
        load_wchunk(1, wsems[1], 0)
        wtargets = [wsems[0].n, wsems[1].n]

        apply_done = {}
        xnT_ev = {}
        prev_tmm = 0
        for t in range(NT):
            sl = t % 2
            xl = self.s_xl[sl]
            w = [(dv, apply_done.get(t - 2, 0)), (self.s_po, po.n if t == 0 else 0)]
            self.sp(lambda h, t=t, sl=sl: h.dma_start(
                out=sx[:, sl, 0:512], in_=self.ytile(t)[:, 0:512]), wait=w, inc=(xl, 16))
            self.sp(lambda h, t=t, sl=sl: h.dma_start(
                out=sx[1:128, sl, 512:1024], in_=self.ytile(t)[0:127, 512:1024]), inc=(xl, 16))
            if t == 0:
                self.pool(lambda h, sl=sl: h.memset(sx[0:1, sl, 512:1024], 0.0), inc=(po, 1))
                bnd = (self.s_po, po.n)
            else:
                self.sp(lambda h, t=t, sl=sl: h.dma_start(
                    out=sx[0:1, sl, 512:1024], in_=self.ytile(t - 1)[127:128, 512:1024]),
                    inc=(xl, 16))
                bnd = (xl, xl.n)
            ld = xl.n
            self.dve(lambda h, sl=sl: h.tensor_scalar(
                out=junkD[:, :], in0=sx[:, sl, :], scalar1=-1.0 / D, scalar2=None,
                op0=ALU.mult, op1=ALU.add, accum_out=CL[:, 28:29]),
                wait=[(xl, ld), bnd], inc=(dv, 1))
            nm = dv.n
            self.act(lambda h, sl=sl: h.activation(
                out=junkA[:, :], in_=sx[:, sl, :], func=AF.Square,
                bias=CL[:, 28:29], accum_out=CL[:, 29:30]),
                wait=[(dv, nm)], inc=(ac, 1))
            self.act(lambda h: h.activation(
                out=CL[:, 30:31], in_=CL[:, 29:30], func=AF.Sqrt,
                scale=1.0 / D, bias=CL[:, 3:4]),
                wait=[(ac, ac.n)], inc=(ac, 1))
            sq = ac.n
            self.dve(lambda h: h.reciprocal(out=CL[:, 31:32], in_=CL[:, 30:31]),
                     wait=[(ac, sq)], inc=(dv, 1))
            rs = dv.n
            self.dve(lambda h, sl=sl: h.tensor_scalar(
                out=sx[:, sl, :], in0=sx[:, sl, :], scalar1=CL[:, 28:29], scalar2=CL[:, 31:32],
                op0=ALU.add, op1=ALU.mult),
                wait=[(dv, rs)], inc=(dv, 1))
            t1d = dv.n
            self.dve(lambda h, sl=sl: h.tensor_tensor(
                out=xnf[:, :], in0=sx[:, sl, :], in1=g_rep[:, :], op=ALU.mult),
                wait=[(dv, t1d), (wl2, wl2_t), (self.s_po, self._xnb_war.get(t - 1, 0) if hasattr(self, "_xnb_war") and (t - 1) in getattr(self, "_xnb_war", {}) else 0)],
                inc=(dv, 1))
            if not hasattr(self, "_xnb_war"):
                self._xnb_war = {}
            xg = dv.n
            self.pool(lambda h: h.tensor_tensor(
                out=xnb[:, :], in0=xnf[:, :], in1=b_rep[:, :], op=ALU.add),
                wait=[(dv, xg), (pe, prev_tmm)], inc=(po, 1))
            xnb_d = po.n
            apply_done[t] = dv.n
            wtr = [(self.s_po, xnb_d)]
            if t >= 1:
                wtr.append((dv if (t - 1) % 2 == 0 else ac, xnT_ev[t - 1]))
            for j in range(8):
                self.pe(lambda h, j=j: h.transpose(
                    self.bankb(6, j * 128, (j + 1) * 128), xnb[:, j * 128:(j + 1) * 128], iB[:, :]),
                    wait=(wtr if j == 0 else None), inc=((pe, 1) if j == 7 else None))
            tmm = pe.n
            prev_tmm = tmm
            self._xnb_war[t] = po.n
            if t % 2 == 0:
                self.dve(lambda h, t=t: h.tensor_copy(
                    out=xnT[:, :, t * 128:(t + 1) * 128],
                    in_=self.bankb(6, 0, 1024).rearrange("p (f j) -> p f j", j=128)),
                    wait=[(pe, tmm)], inc=(dv, 1))
                xnT_ev[t] = dv.n
            else:
                self.act(lambda h, t=t: h.copy(
                    out=xnT[:, :, t * 128:(t + 1) * 128],
                    in_=self.bankb(6, 0, 1024).rearrange("p (f j) -> p f j", j=128)),
                    wait=[(pe, tmm)], inc=(ac, 1))
                xnT_ev[t] = ac.n
        ln_all = (dv.n, ac.n)

        SUBW = [128, 128, 128, 6]
        h_written = {}
        w2_done = {}
        yadd_done = {}
        prev_gelu = 0
        prev_yadd = {}
        for cc in range(NFFC):
            sl = cc % 2
            wsem = wsems[sl]
            wtarget = wtargets[sl]
            if cc >= 2:
                pass
            ep_marks = []
            for tg in range(4):
                for sub in range(4):
                    ws = SUBW[sub]
                    ba = 0 + 2 * (len(ep_marks) % 2)
                    bg = 1 + 2 * (len(ep_marks) % 2)
                    wmm = [(wsem, wtarget)]
                    if sub == 0:
                        wmm.append((dv, xnT_ev.get(4 * tg + 2, 0)))
                        wmm.append((ac, xnT_ev.get(4 * tg + 3, 0)))
                        if tg == 0 and cc >= 1:
                            wmm.append((dv, self._ep_prev.get(cc - 1, 0)))
                    k = len(ep_marks)
                    if k >= 2:
                        wmm.append((dv, ep_marks[k - 2][1]))
                        wmm.append((ac, ep_marks[k - 2][0]))
                    for fc in range(8):
                        for half, bb in ((0, ba), (1, bg)):
                            self.pe(lambda h, fc=fc, half=half, bb=bb, sub=sub, tg=tg, ws=ws, sl=sl: h.matmul(
                                self.bank(bb)[0:ws, :], W1c[sl][:, fc, half, sub * 128:sub * 128 + ws],
                                xnT[:, fc, tg * 512:(tg + 1) * 512],
                                start=(fc == 0), stop=(fc == 7)),
                                wait=(wmm if (fc == 0 and half == 0) else None),
                                inc=((pe, 1) if (fc == 7 and half == 1) else None))
                    mmhg = pe.n
                    gcol = 28 + cc * 4 + sub
                    acol = cc * 4 + sub
                    gslot = (len(ep_marks) % 2) * 512
                    self.act(lambda h, bg=bg, ws=ws, gcol=gcol, gslot=gslot: h.activation(
                        out=xnf[0:ws, gslot:gslot + 512],
                        in_=self.bank(bg)[0:ws, :], func=AF.Gelu, bias=b1c[0:ws, gcol:gcol + 1]),
                        wait=[(pe, mmhg), (dv, prev_gelu)], inc=(ac, 1))
                    gel = ac.n
                    self.dve(lambda h, ba=ba, ws=ws, sub=sub, tg=tg, acol=acol, gslot=gslot: h.scalar_tensor_tensor(
                        out=hbuf[0:ws, sub, tg * 512:(tg + 1) * 512], in0=self.bank(ba)[0:ws, :],
                        scalar=b1c[0:ws, acol:acol + 1], in1=xnf[0:ws, gslot:gslot + 512],
                        op0=ALU.add, op1=ALU.mult),
                        wait=[(ac, gel), (pe, w2_done.get(cc - 1, 0))], inc=(dv, 1))
                    if k >= 1:
                        prev_gelu = ep_marks[k - 1][1] if k < 2 else ep_marks[k - 1][1]
                        prev_gelu = ep_marks[k - 1][1]
                    ep_marks.append((gel, dv.n))
                    if len(ep_marks) >= 2:
                        prev_gelu = ep_marks[len(ep_marks) - 2][1]
            self._ep_prev = getattr(self, "_ep_prev", {})
            self._ep_prev[cc] = dv.n
            h_done = dv.n
            for t in range(NT):
                byo = 4 + 2 * (t % 2)
                wm = [(dv, h_done)]
                if cc == NFFC - 1 and t == 0:
                    wm.append((self.s_cl, ones_cl))
                if t >= 2:
                    wm.append((dv, prev_yadd.get(t - 2, 0)))
                for og in range(2):
                    for sub in range(4):
                        ws = SUBW[sub] + (1 if (sub == 3 and cc == NFFC - 1) else 0)
                        self.pe(lambda h, t=t, og=og, sub=sub, ws=ws, sl=sl, byo=byo: h.matmul(
                            self.bank(byo + og), hbuf[0:ws, sub, t * 128:(t + 1) * 128],
                            W2c[sl][0:ws, sub, og * 512:(og + 1) * 512],
                            start=(sub == 0), stop=(sub == 3)),
                            wait=(wm if (og == 0 and sub == 0) else None),
                            inc=((pe, 1) if (og == 1 and sub == 3) else None))
                mmw2 = pe.n
                for og in range(2):
                    self.dve(lambda h, t=t, og=og, byo=byo: h.tensor_tensor(
                        out=self.ytile(t)[:, og * 512:(og + 1) * 512], in0=self.bank(byo + og),
                        in1=self.ytile(t)[:, og * 512:(og + 1) * 512], op=ALU.add),
                        wait=([(pe, mmw2)] if og == 0 else None), inc=((dv, 1) if og == 1 else None))
                prev_yadd[t] = dv.n
            w2_done[cc] = pe.n
            if cc + 2 < NFFC:
                load_wchunk(cc + 2, wsem, w2_done.get(cc, 0))
                wtargets[sl] = wsem.n
        self._ffn_done = dict(pe=pe.n, dv=dv.n, ac=ac.n)

    def final(self, s):
        pe, dv, ac = self.s_pe, self.s_dv, self.s_ac
        CL = self.cols
        xst = self.ar(111104, F32, 2 * 1024).rearrange("p (s f) -> p s f", f=D)
        ob = self.ar(86528, F32, 1024)
        junkD = self.ar(107008, BF16, 1024)
        junkA = self.ar(109056, BF16, 1024)
        done = {}
        st_prev = {}
        for t in range(NT):
            sl = t % 2
            xl = self.s_xl[sl]
            self.sp(lambda h, t=t, sl=sl: h.dma_start(
                out=xst[:, sl, :], in_=self.x1_scr[t * 128:(t + 1) * 128, :]),
                wait=[(dv, done.get(t - 2, 0))] + ([(self.s_st, self.s_st.n)] if t >= 2 else []), inc=(xl, 16))
            ld = xl.n
            self.dve(lambda h, t=t: h.tensor_scalar(
                out=junkD[:, :], in0=self.ytile(t), scalar1=-1.0 / D, scalar2=None,
                op0=ALU.mult, op1=ALU.add, accum_out=CL[:, 28:29]),
                wait=[(dv, done.get(t - 1, 0))], inc=(dv, 1))
            nm = dv.n
            self.act(lambda h, t=t: h.activation(
                out=junkA[:, :], in_=self.ytile(t), func=AF.Square,
                bias=CL[:, 28:29], accum_out=CL[:, 29:30]),
                wait=[(dv, nm)], inc=(ac, 1))
            self.act(lambda h: h.activation(
                out=CL[:, 30:31], in_=CL[:, 29:30], func=AF.Sqrt,
                scale=1.0 / D, bias=CL[:, 3:4]),
                wait=[(ac, ac.n)], inc=(ac, 1))
            sq = ac.n
            self.dve(lambda h: h.reciprocal(out=CL[:, 31:32], in_=CL[:, 30:31]),
                     wait=[(ac, sq)], inc=(dv, 1))
            rs = dv.n
            self.dve(lambda h, t=t: h.tensor_scalar(
                out=ob[:, :], in0=self.ytile(t), scalar1=CL[:, 28:29], scalar2=CL[:, 31:32],
                op0=ALU.add, op1=ALU.mult),
                wait=[(dv, rs)] + ([(self.s_st, self.s_st.n)] if t >= 1 else []), inc=(dv, 1))
            self.dve(lambda h, sl=sl: h.tensor_tensor(
                out=ob[:, :], in0=ob[:, :], in1=xst[:, sl, :], op=ALU.add),
                wait=[(dv, dv.n), (xl, ld)], inc=(dv, 1))
            done[t] = dv.n
            self.sp(lambda h, t=t, s=s: h.dma_start(
                out=self.out[s, t * 128:(t + 1) * 128, :], in_=ob[:, :]),
                wait=[(dv, done[t])], inc=(self.s_st, 16))
            st_prev[sl] = self.s_st.n
            st_prev["ob"] = self.s_st.n

    def build(self, phases=("all",)):
        self.load_consts()
        for s in range(2):
            self.phase0(s)
            self.barrier()
            if "p0" in phases:
                break
            self.mla(s)
            if "mla" in phases:
                self.barrier()
                break
            self.barrier()
            self.mla_m3(s)
            self.barrier()
            if "m3" in phases:
                break
            stop = False
            for L in range(NL):
                self.attn_layer(s, L)
                self.barrier()
                if f"attn{L}" in phases:
                    self.dump("y", self.y[:, :], [(self.s_dv, self.s_dv.n)])
                    stop = True
                    break
                self.ffn_layer(s, L)
                self.barrier()
                if f"ffn{L}" in phases:
                    self.dump("y", self.y[:, :], [(self.s_dv, self.s_dv.n)])
                    stop = True
                    break
            if stop:
                break
            self.final(s)
            self.barrier()
        self.sp(lambda h: h.wait_ge(self.s_st.h, self.s_st.n))
        self.build_block()


def build_nc(dbg=(), phases=("all",)):
    nc = bass.Bass("TRN2", target_bir_lowering=False, debug=False)
    mk = MK(nc, dbg)
    with mk.es:
        mk.build(phases=phases)
    return nc, mk


def make_in_maps(inputs):
    consts = host_consts()
    b1l = lay_b1(np.asarray(inputs["L0_ffn_b1"], dtype=np.float32))
    gw_bd = np.zeros((D, 32), dtype=np.float32)
    glw = np.asarray(inputs["L0_attn_gate_local_w"], dtype=np.float32)
    grw = np.asarray(inputs["L0_attn_gate_remote_w"], dtype=np.float32)
    for h in range(16):
        gw_bd[h * 64:(h + 1) * 64, h] = glw[h]
        gw_bd[h * 64:(h + 1) * 64, 16 + h] = grw[h]
    shared = {
        "wq": inputs["L0_attn_wq"], "wk": inputs["L0_attn_wk"], "wv": inputs["L0_attn_wv"],
        "wgq": inputs["L0_attn_wgq"], "wgk": inputs["L0_attn_wgk"], "wgv": inputs["L0_attn_wgv"],
        "r_out_w": inputs["L0_attn_r_out_w"], "glw": inputs["L0_attn_gate_local_w"],
        "grw": inputs["L0_attn_gate_remote_w"], "w_out": inputs["L0_attn_w_out"],
        "Wqkv": inputs["L0_ffn_Wqkv"], "qs": inputs["L0_ffn_q_scale"], "ks": inputs["L0_ffn_k_scale"],
        "Wo": inputs["L0_ffn_Wo"], "ln_g": inputs["L0_ffn_ln_g"], "ln_b": inputs["L0_ffn_ln_b"],
        "W1": inputs["L0_ffn_W1"], "b1l": b1l, "W2": inputs["L0_ffn_W2"], "b2": inputs["L0_ffn_b2"],
        "gw_bd": gw_bd,
    }
    shared = {k: np.ascontiguousarray(np.asarray(v, dtype=np.float32)) for k, v in shared.items()}
    shared.update(consts)
    x = np.ascontiguousarray(np.asarray(inputs["L0_attn_x"], dtype=np.float32))
    maps = []
    for c in range(8):
        m = dict(shared)
        m["x_in"] = x[2 * c:2 * c + 2]
        maps.append(m)
    return maps


def kernel(**inputs):
    nc, mk = build_nc()
    maps = make_in_maps(inputs)
    res = run_bass_kernel_spmd(nc, maps, core_ids=list(range(8)))
    out = np.empty((16, T, D), np.float32)
    for c in range(8):
        out[2 * c:2 * c + 2] = np.asarray(res.results[c]["out"])
    return out
